# version 01
# baseline (speedup 1.0000x reference)
import numpy as np
from contextlib import ExitStack
DBG_NOEXTRA = False

import concourse.bass as bass
import concourse.mybir as mybir
from concourse.bass_utils import run_bass_kernel_spmd

F32 = mybir.dt.float32
BF16 = mybir.dt.bfloat16
F16 = mybir.dt.float16
I32 = mybir.dt.int32
AF = mybir.ActivationFunctionType
ALU = mybir.AluOpType
AX = mybir.AxisListType

NCORES = 8
B, S, D = 2, 4096, 2048
NTOK = B * S
TPC = NTOK // NCORES
NT = 512
KD = D // 128


class C:
    def __init__(self, name):
        self.nc = bass.Bass("TRN2", target_bir_lowering=False)
        self.st = ExitStack()
        self.sems = {}
        self.cnt = {}
        self.name = name

    def sem(self, name):
        if name not in self.sems:
            self.sems[name] = self.st.enter_context(self.nc.semaphore(name))
            self.cnt[name] = 0
        return self.sems[name]

    def inc(self, instr, name, k=1):
        instr.then_inc(self.sem(name), k)
        self.cnt[name] += k
        return self.cnt[name]

    def sb(self, name, shape, dtype):
        return self.st.enter_context(self.nc.sbuf_tensor(name, list(shape), dtype))

    def ps(self, name, shape, dtype=F32):
        return self.st.enter_context(self.nc.psum_tensor(name, list(shape), dtype))

    def din(self, name, shape, dtype=F32):
        return self.nc.dram_tensor(name, list(shape), dtype, kind="ExternalInput").ap()

    def dout(self, name, shape, dtype=F32):
        return self.nc.dram_tensor(name, list(shape), dtype, kind="ExternalOutput").ap()

    def dtmp(self, name, shape, dtype=F32):
        return self.nc.dram_tensor(name, list(shape), dtype, kind="Internal").ap()

    def close(self):
        self.st.close()
        self.nc.finalize()
        return self.nc


def make_consts(c):
    c.ones_bf = c.sb("ones_bf", [128, 1], BF16)
    c.ones_f1 = c.sb("ones_f1", [1, 128], F32)
    c.ident_bf = c.sb("ident_bf", [128, 128], BF16)
    c.ident_f = c.sb("ident_f", [128, 128], F32)
    c.iot_j = c.sb("iot_j", [128, 512], F32)
    c.iot_p = c.sb("iot_p", [128, 1], F32)
    nc = c.nc
    with nc.Block() as blk:
        @blk.vector
        def _(v):
            v.memset(c.ones_bf[:], 1.0)
            v.memset(c.ones_f1[:], 1.0)

        @blk.gpsimd
        def _(g):
            g.iota(c.iot_j[:], [[1, 512]], channel_multiplier=0,
                   allow_small_or_imprecise_dtypes=True)
            g.iota(c.iot_p[:], [[1, 1]], channel_multiplier=1,
                   allow_small_or_imprecise_dtypes=True)
    with nc.Block() as blk:
        @blk.vector
        def _(v):
            v.tensor_single_scalar(c.ident_bf[:], c.iot_j[:, 0:128], c.iot_p[:, 0:1],
                                   ALU.is_equal)
            v.tensor_single_scalar(c.ident_f[:], c.iot_j[:, 0:128], c.iot_p[:, 0:1],
                                   ALU.is_equal)


def make_lin_pool(c, kin_max=32, npsum=4):
    c.kin_max = kin_max
    c.wf32 = [c.sb(f"wf32_{i}", [128, kin_max, 128], F32) for i in range(2)]
    c.wbf = [c.sb(f"wbf_{i}", [128, kin_max, 128], BF16) for i in range(2)]
    c.pslin = [c.ps(f"pslin{i}", [128, NT], F32) for i in range(npsum)]
    c.npsum = npsum


def make_norm_pool(c, kmax=KD):
    c.nrm_bf = c.sb("nrm_bf", [128, 2, NT], BF16)
    c.nrm_sq = c.sb("nrm_sq", [128, 2, NT], BF16)
    c.nrm_tmp = c.sb("nrm_tmp", [128, NT], F32)
    c.rows = c.sb("rows", [1, 2 * NT], F32)
    c.ps_rowA = c.ps("ps_rowA", [1, NT], F32)
    c.ps_rowB = c.ps("ps_rowB", [1, NT], F32)
    c.ps_bA = c.ps("ps_bA", [128, NT], F32)
    c.ps_bB = c.ps("ps_bB", [128, NT], F32)


def emit_norm(c, src3, t0, nt, K, kind="ln", eps=1e-5, dst3=None, dtt0=0,
              gamma=None, beta=None, resid3=None, rt0=0):
    nc = c.nc
    Dtot = K * 128
    mean_r = c.rows[:, 0:nt]
    ex2_r = c.rows[:, NT:NT + nt]
    var_r = ex2_r
    rstd_r = ex2_r
    need_mean = (kind == "ln")
    with nc.Block() as blk:
        base_cast = c.cnt.get("n_cast", 0)
        base_sq = c.cnt.get("n_sq", 0)
        base_mm = c.cnt.get("n_mm", 0)

        @blk.vector
        def _(v):
            if need_mean:
                for k in range(K):
                    if k >= 2:
                        v.wait_ge(c.sem("n_mm"), base_mm + k - 1)
                    ins = v.tensor_copy(c.nrm_bf[:, k % 2, 0:nt],
                                        src3[:, k, t0:t0 + nt])
                    c.inc(ins, "n_cast")

        @blk.scalar
        def _(sc):
            for k in range(K):
                if k >= 2:
                    sc.wait_ge(c.sem("n_mm"), base_mm + k - 1)
                ins = sc.activation(c.nrm_sq[:, k % 2, 0:nt], src3[:, k, t0:t0 + nt],
                                    AF.Square)
                c.inc(ins, "n_sq")

        @blk.tensor
        def _(te):
            for k in range(K):
                te.wait_ge(c.sem("n_sq"), base_sq + k + 1)
                if need_mean:
                    te.wait_ge(c.sem("n_cast"), base_cast + k + 1)
                    te.matmul(c.ps_rowA[:, 0:nt], c.ones_bf[:],
                              c.nrm_bf[:, k % 2, 0:nt],
                              start=(k == 0), stop=(k == K - 1))
                ins = te.matmul(c.ps_rowB[:, 0:nt], c.ones_bf[:],
                                c.nrm_sq[:, k % 2, 0:nt],
                                start=(k == 0), stop=(k == K - 1))
                c.inc(ins, "n_mm")
    with nc.Block() as blk:
        @blk.vector
        def _(v):
            v.tensor_single_scalar(ex2_r, c.ps_rowB[:, 0:nt], 1.0 / Dtot, ALU.mult)
            if need_mean:
                m2_r = c.nrm_tmp[0:1, 0:nt]
                v.tensor_single_scalar(mean_r, c.ps_rowA[:, 0:nt], 1.0 / Dtot, ALU.mult)
                v.tensor_tensor(m2_r, mean_r, mean_r, ALU.mult)
                v.tensor_tensor(var_r, ex2_r, m2_r, ALU.subtract)
            else:
                v.tensor_copy(var_r, ex2_r)
    with nc.Block() as blk:
        @blk.vector
        def _(v):
            v.tensor_single_scalar(var_r, var_r, float(eps), ALU.add)
    with nc.Block() as blk:
        @blk.scalar
        def _(sc):
            sc.activation(rstd_r, var_r, AF.Ln)
            sc.activation(rstd_r, rstd_r, AF.Exp, scale=-0.5)
    with nc.Block() as blk:
        @blk.tensor
        def _(te):
            if need_mean:
                te.matmul(c.ps_bA[:, 0:nt], c.ones_f1[:], mean_r, start=True, stop=True)
            te.matmul(c.ps_bB[:, 0:nt], c.ones_f1[:], rstd_r, start=True, stop=True)
    with nc.Block() as blk:
        @blk.vector
        def _(v):
            for k in range(K):
                if need_mean:
                    v.tensor_tensor(c.nrm_tmp[:, 0:nt], src3[:, k, t0:t0 + nt],
                                    c.ps_bA[:, 0:nt], ALU.subtract)
                    srck = c.nrm_tmp[:, 0:nt]
                else:
                    srck = src3[:, k, t0:t0 + nt]
                if resid3 is not None:
                    v.tensor_tensor(c.nrm_tmp[:, 0:nt], srck, c.ps_bB[:, 0:nt],
                                    ALU.mult)
                    v.tensor_tensor(resid3[:, k, rt0:rt0 + nt],
                                    resid3[:, k, rt0:rt0 + nt],
                                    c.nrm_tmp[:, 0:nt], ALU.add)
                if dst3 is not None:
                    if gamma is not None:
                        v.tensor_tensor(c.nrm_tmp[:, 0:nt], srck, c.ps_bB[:, 0:nt],
                                        ALU.mult)
                        if beta is None:
                            v.tensor_single_scalar(dst3[:, k, dtt0:dtt0 + nt],
                                                   c.nrm_tmp[:, 0:nt],
                                                   gamma[:, k:k + 1], ALU.mult)
                        else:
                            v.tensor_scalar(dst3[:, k, dtt0:dtt0 + nt],
                                            c.nrm_tmp[:, 0:nt],
                                            gamma[:, k:k + 1], beta[:, k:k + 1],
                                            ALU.mult, ALU.add)
                    else:
                        v.tensor_tensor(dst3[:, k, dtt0:dtt0 + nt], srck,
                                        c.ps_bB[:, 0:nt], ALU.mult)


def emit_linear(c, wdram, x3, KIN, NOUT_C, nt, xt0=0, x_k0=0, OC0=0,
                dst3=None, dtt0=0, func=AF.Identity, bias2=None, scale=1.0,
                evac_cb=None, evac_engine="scalar", w_bf16=False, extra=None,
                dst_t0=0):
    nc = c.nc
    kin_bufmax = min(c.kin_max, KIN)
    nsplit = (KIN + kin_bufmax - 1) // kin_bufmax
    with nc.Block() as blk:
        base_ld = c.cnt.get("l_ld", 0)
        base_cast = c.cnt.get("l_cast", 0)
        base_mm = c.cnt.get("l_mm", 0)
        base_ev = c.cnt.get("l_ev", 0)

        def _one_evac(eng, oc):
            pi = oc % c.npsum
            eng.wait_ge(c.sem("l_mm"), base_mm + (oc + 1) * nsplit)
            if evac_cb is not None:
                ins = evac_cb(eng, oc, c.pslin[pi][:, 0:nt])
            else:
                kw = {}
                if bias2 is not None:
                    kw["bias"] = bias2[:, OC0 + oc:OC0 + oc + 1]
                ins = eng.activation(dst3[:, dtt0 + oc, dst_t0:dst_t0 + nt],
                                     c.pslin[pi][:, 0:nt], func,
                                     scale=scale, **kw)
            c.inc(ins, "l_ev", 1)

        _vec_evac = _one_evac

        @blk.sync
        def _(sy):
            it = 0
            for oc in range(NOUT_C):
                for sp in range(nsplit):
                    k0 = sp * kin_bufmax
                    kn = min(kin_bufmax, KIN - k0)
                    if it >= 2:
                        sy.wait_ge(c.sem("l_cast" if not w_bf16 else "l_mm"),
                                   (base_cast + it - 1) if not w_bf16
                                   else (base_mm + it - 1))
                    src = wdram[k0 * 128:(k0 + kn) * 128,
                                (OC0 + oc) * 128:(OC0 + oc) * 128 + 128]
                    src = src.rearrange("(k p) o -> p k o", p=128)
                    dstw = c.wbf[it % 2] if w_bf16 else c.wf32[it % 2]
                    ins = sy.dma_start(dstw[:, 0:kn, :], src)
                    c.inc(ins, "l_ld", 16)
                    it += 1

        @blk.vector
        def _(v):
            it = 0
            for oc in range(NOUT_C):
                for sp in range(nsplit):
                    k0 = sp * kin_bufmax
                    kn = min(kin_bufmax, KIN - k0)
                    if not w_bf16:
                        v.wait_ge(c.sem("l_ld"), base_ld + 16 * (it + 1))
                        if it >= 2:
                            v.wait_ge(c.sem("l_mm"), base_mm + it - 1)
                        ins = v.tensor_copy(c.wbf[it % 2][:, 0:kn, :],
                                            c.wf32[it % 2][:, 0:kn, :])
                        c.inc(ins, "l_cast", 1)
                    it += 1
                if evac_engine == "vector" and oc >= 1:
                    _vec_evac(v, oc - 1)
            if evac_engine == "vector":
                _vec_evac(v, NOUT_C - 1)

        @blk.tensor
        def _(te):
            it = 0
            for oc in range(NOUT_C):
                pi = oc % c.npsum
                ins = None
                for sp in range(nsplit):
                    k0 = sp * kin_bufmax
                    kn = min(kin_bufmax, KIN - k0)
                    if w_bf16:
                        te.wait_ge(c.sem("l_ld"), base_ld + 16 * (it + 1))
                    else:
                        te.wait_ge(c.sem("l_cast"), base_cast + it + 1)
                    if sp == 0 and oc >= c.npsum:
                        te.wait_ge(c.sem("l_ev"), base_ev + oc - c.npsum + 1)
                    last = (sp == nsplit - 1)
                    for ic in range(kn):
                        ins = te.matmul(c.pslin[pi][:, 0:nt],
                                        c.wbf[it % 2][:, ic, :],
                                        x3[:, x_k0 + k0 + ic, xt0:xt0 + nt],
                                        start=(sp == 0 and ic == 0),
                                        stop=(last and ic == kn - 1
                                              and extra is None))
                    if last and extra is not None:
                        exl, exr, exn = extra
                        ins = te.matmul(c.pslin[pi][:, 0:nt],
                                        exl[0:exn, oc, :], exr[0:exn, 0:nt],
                                        start=False, stop=True)
                    c.inc(ins, "l_mm", 1)
                    it += 1

        if evac_engine == "scalar":
            @blk.scalar
            def _(sc):
                for oc in range(NOUT_C):
                    _one_evac(sc, oc)


def dma_in_act(c, sy, dst3, src2, t0, nt, sem="xld"):
    ins = sy.dma_start(dst3[:, :, 0:nt],
                       src2[:, t0:t0 + nt].rearrange("(k p) t -> p k t", p=128))
    return c.inc(ins, sem, 16)


def _cp_to_dbg(c, src3, dbgbuf, K):
    with c.nc.Block() as blk:
        @blk.vector
        def _(v):
            for k in range(K):
                v.tensor_copy(dbgbuf[:, k, 0:NT], src3[:, k, 0:NT])


def _store_dbg(c, buf3, xout, t0, dbg):
    with c.nc.Block() as blk:
        @blk.gpsimd
        def _(g):
            ins = g.dma_start(
                xout[:, t0:t0 + NT].rearrange("(k p) t -> p k t", p=128),
                buf3[:, :, 0:NT])
            c.inc(ins, "xst", 16)
            g.wait_ge(c.sem("xst"), c.cnt["xst"])


def build_l2f(upto=99, dbg=None):
    c = C("l2f")
    nc = c.nc
    xin = c.din("xin", [D, TPC])
    up_w = c.din("up_w", [D, 8192])
    down_w = c.din("down_w", [8192, D])
    expert_w = c.din("expert_w", [D, D])
    gate_w = c.din("gate_w", [D, 8])
    ln_g = c.din("ln_g", [128, KD])
    ln_b = c.din("ln_b", [128, KD])
    xout = c.dout("xout", [D, TPC])

    make_consts(c)
    make_lin_pool(c, kin_max=16, npsum=4)
    make_norm_pool(c)

    KH = 8192 // 128
    xf = c.sb("xf", [128, KD, NT], F32)
    xn = c.sb("xn", [128, KD, NT], BF16)
    hb = c.sb("hb", [128, KH, NT], BF16)
    yb = c.sb("yb", [128, KD, NT], F32)
    zb = xn
    gwb = c.sb("gwb", [128, KD, 8], BF16)
    gamma2 = c.sb("gamma2", [128, KD], F32)
    beta2 = c.sb("beta2", [128, KD], F32)
    sc_tok = c.sb("sc_tok", [128, 4], F32)
    sc_row = c.sb("sc_row", [1, NT], F32)
    tmq = c.sb("tmq", [128, 8], F32)
    tm_m = c.sb("tm_m", [128, 2], F32)
    scb = c.sb("scb", [128, NT], F32)

    with nc.Block() as blk:
        @blk.sync
        def _(sy):
            c.inc(sy.dma_start(gamma2[:, :], ln_g[:, :]), "init", 16)
            c.inc(sy.dma_start(beta2[:, :], ln_b[:, :]), "init", 16)
            c.inc(sy.dma_start(c.wf32[0][:, 0:KD, 0:8],
                               gate_w.rearrange("(k p) o -> p k o", p=128)), "init", 16)

        @blk.vector
        def _(v):
            v.wait_ge(c.sem("init"), c.cnt["init"])
            v.tensor_copy(gwb[:, :, :], c.wf32[0][:, 0:KD, 0:8])

    dbgbuf = yb
    for tt in range(TPC // NT):
        t0 = tt * NT
        with nc.Block() as blk:
            @blk.sync
            def _(sy):
                dma_in_act(c, sy, xf, xin, t0, NT)

            @blk.vector
            def _(v):
                v.wait_ge(c.sem("xld"), c.cnt["xld"])
        if upto < 1:
            _store_dbg(c, xf, xout, t0, dbg)
            continue
        emit_norm(c, xf, 0, NT, KD, kind="ln", eps=1e-5, dst3=xn)
        if upto < 2:
            _cp_to_dbg(c, xn, dbgbuf, KD)
            _store_dbg(c, dbgbuf, xout, t0, dbg)
            continue

        with nc.Block() as blk:
            base_mm = c.cnt.get("rt_mm", 0)
            base_dn = c.cnt.get("rt_dn", 0)

            @blk.tensor
            def _(te):
                for q in range(NT // 128):
                    if q >= c.npsum:
                        te.wait_ge(c.sem("rt_dn"), base_dn + q - c.npsum + 1)
                    for k in range(KD):
                        ins = te.matmul(c.pslin[q % c.npsum][:, 0:8],
                                        xn[:, k, q * 128:(q + 1) * 128],
                                        gwb[:, k, :], start=(k == 0),
                                        stop=(k == KD - 1))
                    c.inc(ins, "rt_mm", 1)

            @blk.vector
            def _(v):
                for q in range(NT // 128):
                    v.wait_ge(c.sem("rt_mm"), base_mm + q + 1)
                    ins = v.tensor_copy(tmq[:, :], c.pslin[q % c.npsum][:, 0:8])
                    c.inc(ins, "vss"); v.wait_ge(c.sem("vss"), c.cnt["vss"])
                    if upto == 2:
                        v.tensor_copy(dbgbuf[:, 4 + q, 0:8], tmq[:, :])
                    ins = v.reduce_max(tm_m[:, 0:1], tmq[:, :], axis=AX.X)
                    c.inc(ins, "vss"); v.wait_ge(c.sem("vss"), c.cnt["vss"])
                    ins = v.tensor_single_scalar(c.nrm_tmp[:, 0:8], tmq[:, :],
                                           tm_m[:, 0:1], ALU.is_ge)
                    c.inc(ins, "vss"); v.wait_ge(c.sem("vss"), c.cnt["vss"])
                    if upto == 2:
                        v.tensor_copy(dbgbuf[:, 8 + q, 0:8], c.nrm_tmp[:, 0:8])
                    ins = v.scalar_tensor_tensor(c.nrm_tmp[:, 0:8], c.nrm_tmp[:, 0:8],
                                           -1e30, tmq[:, :], ALU.mult, ALU.add)
                    c.inc(ins, "vss"); v.wait_ge(c.sem("vss"), c.cnt["vss"])
                    ins = v.reduce_max(tm_m[:, 1:2], c.nrm_tmp[:, 0:8], axis=AX.X)
                    c.inc(ins, "vss"); v.wait_ge(c.sem("vss"), c.cnt["vss"])
                    ins = v.tensor_tensor(sc_tok[:, q:q + 1], tm_m[:, 0:1],
                                          tm_m[:, 1:2], ALU.add)
                    c.inc(ins, "rt_dn", 1)
        with nc.Block() as blk:
            @blk.tensor
            def _(te):
                for q in range(NT // 128):
                    te.transpose(c.pslin[q][0:1, 0:128], sc_tok[:, q:q + 1],
                                 c.ident_f[:])
        with nc.Block() as blk:
            @blk.vector
            def _(v):
                for q in range(NT // 128):
                    v.tensor_copy(sc_row[:, q * 128:(q + 1) * 128],
                                  c.pslin[q][0:1, 0:128])
        with nc.Block() as blk:
            @blk.tensor
            def _(te):
                te.matmul(c.ps_bA[:, 0:NT], c.ones_f1[:], sc_row[:, 0:NT],
                          start=True, stop=True)
        with nc.Block() as blk:
            @blk.vector
            def _(v):
                v.tensor_copy(scb[:, :], c.ps_bA[:, 0:NT])

        if upto < 3:
            with nc.Block() as blk:
                @blk.vector
                def _(v):
                    v.tensor_copy(dbgbuf[:, 0, 0:4], sc_tok[:, 0:4])
                    v.tensor_copy(dbgbuf[:, 1, 0:NT], scb[:, 0:NT])
                    v.tensor_copy(dbgbuf[0:1, 2, 0:NT], sc_row[:, 0:NT])
                    for q in range(4):
                        v.tensor_copy(dbgbuf[0:1, 3, q*128:(q+1)*128],
                                      c.pslin[q][0:1, 0:128])
            _store_dbg(c, dbgbuf, xout, t0, dbg)
            continue
        def evac_relu(eng, oc, psum):
            return eng.activation(hb[:, oc, 0:NT], psum, AF.Relu)
        emit_linear(c, up_w, xn, KD, KH, NT, evac_cb=evac_relu)
        with nc.Block() as blk:
            @blk.vector
            def _(v):
                for oc in range(KH):
                    v.tensor_tensor(hb[:, oc, 0:NT], hb[:, oc, 0:NT],
                                    hb[:, oc, 0:NT], ALU.mult)

        if upto < 4:
            _cp_to_dbg(c, hb, dbgbuf, KD)
            _store_dbg(c, dbgbuf, xout, t0, dbg)
            continue
        def evac_moe(eng, oc, psum):
            return eng.tensor_tensor(yb[:, oc, 0:NT], psum, scb[:, 0:NT], ALU.mult)
        emit_linear(c, expert_w, xn, KD, KD, NT, evac_cb=evac_moe,
                    evac_engine="vector")

        def evac_down(eng, oc, psum):
            return eng.tensor_tensor(yb[:, oc, 0:NT], psum, yb[:, oc, 0:NT], ALU.add)
        emit_linear(c, down_w, hb, KH, KD, NT, evac_cb=evac_down,
                    evac_engine="vector")

        if upto < 5:
            _cp_to_dbg(c, yb, dbgbuf, KD)
            _store_dbg(c, dbgbuf, xout, t0, dbg)
            continue
        emit_norm(c, yb, 0, NT, KD, kind="ln", eps=1e-5, dst3=zb,
                  gamma=gamma2, beta=beta2)
        emit_norm(c, zb, 0, NT, KD, kind="ln", eps=1e-5, resid3=xf)

        with nc.Block() as blk:
            @blk.gpsimd
            def _(g):
                ins = g.dma_start(
                    xout[:, t0:t0 + NT].rearrange("(k p) t -> p k t", p=128),
                    xf[:, :, 0:NT])
                c.inc(ins, "xst", 16)
                g.wait_ge(c.sem("xst"), c.cnt["xst"])
    return c.close()


def _ppart(vec):
    v = np.asarray(vec).reshape(-1, 128)
    return np.ascontiguousarray(v.T)


def run_l2f(XT, inp, upto=99, dbg=None):
    nc = build_l2f(upto, dbg)
    in_maps = []
    for cid in range(NCORES):
        t0 = cid * TPC
        in_maps.append({
            "xin": np.ascontiguousarray(XT[:, t0:t0 + TPC]),
            "up_w": inp["L2_ffn_up_w"],
            "down_w": inp["L2_ffn_down_w"],
            "expert_w": inp["L2_ffn_expert_weight"],
            "gate_w": inp["L2_ffn_gate_w"],
            "ln_g": _ppart(inp["L2_ffn_ln_gamma"]),
            "ln_b": _ppart(inp["L2_ffn_ln_beta"]),
        })
    res = run_bass_kernel_spmd(nc, in_maps, core_ids=list(range(NCORES)))
    out = np.concatenate([res.results[i]["xout"] for i in range(NCORES)], axis=1)
    return out


def sandwich_store(c, src3, xf, xout, t0, K=KD):
    emit_norm(c, src3, 0, NT, K, kind="ln", eps=1e-5, resid3=xf)
    with c.nc.Block() as blk:
        @blk.gpsimd
        def _(g):
            ins = g.dma_start(
                xout[:, t0:t0 + NT].rearrange("(k p) t -> p k t", p=128),
                xf[:, :, 0:NT])
            c.inc(ins, "xst", 16)
            g.wait_ge(c.sem("xst"), c.cnt["xst"])


def build_l3w():
    c = C("l3w")
    nc = c.nc
    DSL = 256
    w2 = c.din("w2", [DSL, 8192])
    h128 = c.din("h128", [128, 128])
    h64 = c.din("h64", [64, 64])
    out = c.dout("w2pt", [8192, DSL], BF16)
    w2bf = c.dtmp("w2bf", [DSL, 8192], BF16)
    t1 = c.dtmp("t1", [8192, DSL], BF16)

    xf = c.sb("xf", [128, 2, 8192], F32)
    xb = c.sb("xb", [128, 2, 8192], BF16)
    w2t = c.sb("w2t", [128, 64, DSL], BF16)
    a1 = c.sb("a1", [128, 64, DSL], BF16)
    hb128 = c.sb("hb128", [128, 128], BF16)
    hb64 = c.sb("hb64", [64, 64], BF16)
    hf = c.sb("hf", [128, 128], F32)
    st2 = c.sb("st2", [64, 64, DSL], BF16)
    ps = c.ps("ps", [128, DSL], F32)
    ps2 = c.ps("ps2", [128, DSL], F32)

    with nc.Block() as blk:
        @blk.sync
        def _(sy):
            c.inc(sy.dma_start(xf[:, :, :], w2.rearrange("(q p) f -> p q f", p=128)),
                  "ld", 16)
            c.inc(sy.dma_start(hf[:, :], h128[:, :]), "ld", 16)

        @blk.vector
        def _(v):
            v.wait_ge(c.sem("ld"), c.cnt["ld"])
            v.tensor_copy(xb[:, :, :], xf[:, :, :])
            v.tensor_copy(hb128[:, :], hf[:, :])
    with nc.Block() as blk:
        @blk.sync
        def _(sy):
            c.inc(sy.dma_start(hf[0:64, 0:64], h64[:, :]), "ld", 16)

        @blk.vector
        def _(v):
            v.wait_ge(c.sem("ld"), c.cnt["ld"])
            v.tensor_copy(hb64[:, :], hf[0:64, 0:64])
    with nc.Block() as blk:
        @blk.gpsimd
        def _(g):
            ins = g.dma_start(w2bf.rearrange("(q p) f -> p q f", p=128), xb[:, :, :])
            c.inc(ins, "st", 16)
    with nc.Block() as blk:
        @blk.sync
        def _(sy):
            sy.wait_ge(c.sem("st"), c.cnt["st"])
            for p in range(64):
                ins = sy.dma_start_transpose(w2t[:, p, :],
                                             w2bf[:, p * 128:(p + 1) * 128])
                c.inc(ins, "ld", 16)

        @blk.vector
        def _(v):
            v.wait_ge(c.sem("ld"), c.cnt["ld"])
    with nc.Block() as blk:
        base_mm1 = c.cnt.get("mm1", 0)
        base_ev1 = c.cnt.get("ev1", 0)

        @blk.tensor
        def _(te):
            for p in range(64):
                if p >= 2:
                    te.wait_ge(c.sem("ev1"), base_ev1 + p - 1)
                ins = te.matmul(([ps, ps2][p % 2])[:, :], hb128[:, :], w2t[:, p, :],
                                start=True, stop=True)
                c.inc(ins, "mm1", 1)

        @blk.scalar
        def _(sc):
            for p in range(64):
                sc.wait_ge(c.sem("mm1"), base_mm1 + p + 1)
                ins = sc.activation(a1[:, p, :], ([ps, ps2][p % 2])[:, :], AF.Identity)
                c.inc(ins, "ev1", 1)
    with nc.Block() as blk:
        @blk.gpsimd
        def _(g):
            ins = g.dma_start(t1.rearrange("(q p) d -> p q d", p=128), a1[:, :, :])
            c.inc(ins, "st", 16)
            g.wait_ge(c.sem("st"), c.cnt["st"])
    SCL = 1.0 / float(np.sqrt(8192.0))
    out4 = out.rearrange("(q hh i) d -> hh q i d", q=64, hh=2, i=64)
    with nc.Block() as blk:
        base_ld2 = c.cnt.get("ld2", 0)
        base_mm2 = c.cnt.get("mm2", 0)
        base_ev2 = c.cnt.get("ev2", 0)
        base_st2 = c.cnt.get("st2d", 0)

        @blk.sync
        def _(sy):
            for i in range(128):
                if i >= 2:
                    sy.wait_ge(c.sem("mm2"), base_mm2 + i - 1)
                ins = sy.dma_start(w2t[0:64, i % 2, :], t1[i:8192:128, :])
                c.inc(ins, "ld2", 16)

        @blk.tensor
        def _(te):
            for i in range(128):
                te.wait_ge(c.sem("ld2"), base_ld2 + 16 * (i + 1))
                if i >= 2:
                    te.wait_ge(c.sem("ev2"), base_ev2 + i - 1)
                ins = te.matmul(([ps, ps2][i % 2])[0:64, :], hb64[:, :],
                                w2t[0:64, i % 2, :], start=True, stop=True)
                c.inc(ins, "mm2", 1)

        @blk.scalar
        def _(sc):
            for i in range(128):
                if i == 64:
                    sc.wait_ge(c.sem("st2d"), base_st2 + 16)
                sc.wait_ge(c.sem("mm2"), base_mm2 + i + 1)
                ins = sc.activation(st2[0:64, i % 64, :], ([ps, ps2][i % 2])[0:64, :],
                                    AF.Identity, scale=SCL)
                c.inc(ins, "ev2", 1)

        @blk.gpsimd
        def _(g):
            for half in range(2):
                g.wait_ge(c.sem("ev2"), base_ev2 + (half + 1) * 64)
                ins = g.dma_start(out4[half], st2[0:64, :, :])
                c.inc(ins, "st2d", 16)
            g.wait_ge(c.sem("st2d"), c.cnt["st2d"])
    return c.close()


def build_l3f():
    c = C("l3f")
    nc = c.nc
    xin = c.din("xin", [D, TPC])
    w1t = c.din("w1t", [D, 8192])
    w3t = c.din("w3t", [D, 8192])
    w2pt = c.din("w2pt", [8192, D], BF16)
    xout = c.dout("xout", [D, TPC])

    make_consts(c)
    make_lin_pool(c, kin_max=16, npsum=4)
    make_norm_pool(c)

    KH = 64
    xf = c.sb("xf", [128, KD, NT], F32)
    xn = c.sb("xn", [128, KD, NT], BF16)
    hb = c.sb("hb", [128, KH, NT], BF16)
    yb = c.sb("yb", [128, KD, NT], F32)

    for tt in range(TPC // NT):
        t0 = tt * NT
        with nc.Block() as blk:
            @blk.sync
            def _(sy):
                dma_in_act(c, sy, xf, xin, t0, NT)

            @blk.vector
            def _(v):
                v.wait_ge(c.sem("xld"), c.cnt["xld"])
        emit_norm(c, xf, 0, NT, KD, kind="ln", eps=1e-5, dst3=xn)

        def evac_silu(eng, oc, psum):
            return eng.activation(hb[:, oc, 0:NT], psum, AF.Silu)
        emit_linear(c, w1t, xn, KD, KH, NT, evac_cb=evac_silu)

        def evac_mul(eng, oc, psum):
            return eng.tensor_tensor(hb[:, oc, 0:NT], psum, hb[:, oc, 0:NT], ALU.mult)
        emit_linear(c, w3t, xn, KD, KH, NT, evac_cb=evac_mul, evac_engine="vector")

        def evac_y(eng, oc, psum):
            return eng.activation(yb[:, oc, 0:NT], psum, AF.Identity)
        emit_linear(c, w2pt, hb, KH, KD, NT, evac_cb=evac_y, w_bf16=True)

        sandwich_store(c, yb, xf, xout, t0)
    return c.close()


def run_l3w(inp):
    nc = build_l3w()
    Hm = {1: np.array([[1.0]], np.float32)}
    def had(n):
        if n == 1:
            return np.array([[1.0]], np.float32)
        h = had(n // 2)
        return np.block([[h, h], [h, -h]]).astype(np.float32)
    H128 = had(128)
    H64 = had(64)
    w2 = inp["L3_ffn_W2"]
    in_maps = []
    for cid in range(NCORES):
        in_maps.append({
            "w2": np.ascontiguousarray(w2[cid * 256:(cid + 1) * 256, :]),
            "h128": H128, "h64": H64,
        })
    res = run_bass_kernel_spmd(nc, in_maps, core_ids=list(range(NCORES)))
    return np.concatenate([res.results[i]["w2pt"] for i in range(NCORES)], axis=1)


def run_l3f(XT, inp, w2pt):
    nc = build_l3f()
    w1t = np.ascontiguousarray(inp["L3_ffn_W1"].T)
    w3t = np.ascontiguousarray(inp["L3_ffn_W3"].T)
    in_maps = []
    for cid in range(NCORES):
        t0 = cid * TPC
        in_maps.append({
            "xin": np.ascontiguousarray(XT[:, t0:t0 + TPC]),
            "w1t": w1t, "w3t": w3t, "w2pt": w2pt,
        })
    res = run_bass_kernel_spmd(nc, in_maps, core_ids=list(range(NCORES)))
    return np.concatenate([res.results[i]["xout"] for i in range(NCORES)], axis=1)


def build_l0f(upto=99):
    c = C("l0f")
    nc = c.nc
    KH = 43
    xin = c.din("xin", [D, TPC])
    w_gate = c.din("w_gate", [D, 5504])
    w_up = c.din("w_up", [D, 5504])
    w_down = c.din("w_down", [5504, D])
    w_rt = c.din("w_rt", [D, 8])
    acat_g = c.din("acat_g", [D, 72])
    acat_u = c.din("acat_u", [D, 72])
    acat_d = c.din("acat_d", [5504, 72])
    bcat_g = c.din("bcat_g", [72, 5504])
    bcat_u = c.din("bcat_u", [72, 5504])
    bcat_d = c.din("bcat_d", [72, D])
    ebias = c.din("ebias", [128, 8])
    e8 = c.din("e8", [8, 64])
    xout = c.dout("xout", [D, TPC])

    make_consts(c)
    make_lin_pool(c, kin_max=16, npsum=4)
    make_norm_pool(c)

    xf = c.sb("xf", [128, KD, NT], F32)
    xn = c.sb("xn", [128, KD, NT], BF16)
    hb = c.sb("hb", [128, KH, NT], BF16)
    yb = c.sb("yb", [128, KD, NT], F32)
    acgb = c.sb("acgb", [128, KD, 72], BF16)
    acub = c.sb("acub", [128, KD, 72], BF16)
    acdb = c.sb("acdb", [128, KH, 72], BF16)
    bcgb = c.sb("bcgb", [72, KH, 128], BF16)
    bcub = c.sb("bcub", [72, KH, 128], BF16)
    bcdb = c.sb("bcdb", [72, KD, 128], BF16)
    rwb = c.sb("rwb", [128, KD, 8], BF16)
    e8b = c.sb("e8b", [8, 64], F32)
    ebt = c.sb("ebt", [128, 8], F32)
    lg = c.sb("lg", [128, 8], F32)
    m12 = c.sb("m12", [128, 2], F32)
    msk = c.sb("msk", [128, 8], F32)
    wr8 = c.sb("wr8", [128, 8], F32)
    wtok = c.sb("wtok", [128, 8], F32)
    w8 = c.sb("w8", [8, NT], F32)
    wexp = c.sb("wexp", [128, NT], F32)
    xawg = c.sb("xawg", [72, NT], BF16)
    xawu = c.sb("xawu", [72, NT], BF16)
    xawd = c.sb("xawd", [72, NT], BF16)

    def preload(dst3, dram, K, w):
        for k0 in range(0, K, 16):
            kn = min(16, K - k0)
            with nc.Block() as blk:
                @blk.sync
                def _(sy):
                    c.inc(sy.dma_start(
                        c.wf32[0][:, 0:kn, 0:w],
                        dram[k0 * 128:(k0 + kn) * 128, :].rearrange(
                            "(k p) o -> p k o", p=128)), "init", 16)

                @blk.vector
                def _(v):
                    v.wait_ge(c.sem("init"), c.cnt["init"])
                    v.tensor_copy(dst3[:, k0:k0 + kn, 0:w],
                                  c.wf32[0][:, 0:kn, 0:w])

    def preload72(dst3, dram, K):
        for k0 in range(0, K, 16):
            kn = min(16, K - k0)
            with nc.Block() as blk:
                @blk.sync
                def _(sy):
                    c.inc(sy.dma_start(
                        c.wf32[0][0:72, 0:kn, :],
                        dram[:, k0 * 128:(k0 + kn) * 128].rearrange(
                            "e (k o) -> e k o", o=128)), "init", 16)

                @blk.vector
                def _(v):
                    v.wait_ge(c.sem("init"), c.cnt["init"])
                    v.tensor_copy(dst3[0:72, k0:k0 + kn, :],
                                  c.wf32[0][0:72, 0:kn, :])

    if upto < 2:
        dbg_rw = c.dout("dbg_rw", [128, 2, 128])
    preload(acgb, acat_g, KD, 72)
    preload(acub, acat_u, KD, 72)
    preload(acdb, acat_d, KH, 72)
    preload(rwb, w_rt, KD, 8)
    if upto < 2:
        with nc.Block() as blk:
            @blk.vector
            def _(v):
                for k in range(KD):
                    v.tensor_copy(c.nrm_bf[:, 0, k * 8:(k + 1) * 8], rwb[:, k, :])
        with nc.Block() as blk:
            @blk.gpsimd
            def _(g):
                c.inc(g.dma_start(dbg_rw[:, 0, :], c.nrm_bf[:, 0, 0:128]), "dbgs", 16)
                g.wait_ge(c.sem("dbgs"), c.cnt["dbgs"])
    preload72(bcgb, bcat_g, KH)
    preload72(bcub, bcat_u, KH)
    preload72(bcdb, bcat_d, KD)
    with nc.Block() as blk:
        @blk.sync
        def _(sy):
            c.inc(sy.dma_start(ebt[:, :], ebias[:, :]), "init", 16)
            c.inc(sy.dma_start(c.wf32[0][0:8, 0, 0:64], e8[:, :]), "init", 16)

        @blk.vector
        def _(v):
            v.wait_ge(c.sem("init"), c.cnt["init"])
            v.tensor_copy(e8b[:, :], c.wf32[0][0:8, 0, 0:64])

    for tt in range(TPC // NT):
        t0 = tt * NT
        with nc.Block() as blk:
            @blk.sync
            def _(sy):
                dma_in_act(c, sy, xf, xin, t0, NT)

            @blk.vector
            def _(v):
                v.wait_ge(c.sem("xld"), c.cnt["xld"])
        emit_norm(c, xf, 0, NT, KD, kind="ln", eps=1e-5, dst3=xn)

        with nc.Block() as blk:
            base_mm = c.cnt.get("rt_mm", 0)
            base_sg = c.cnt.get("rt_sg", 0)
            base_dn = c.cnt.get("rt_dn", 0)
            base_tr = c.cnt.get("rt_tr", 0)
            base_w8 = c.cnt.get("rt_w8", 0)
            base_cp = c.cnt.get("rt_cp", 0)
            NQ = NT // 128

            @blk.tensor
            def _(te):
                for q in range(NQ):
                    if q >= 3:
                        te.wait_ge(c.sem("rt_dn"), base_dn + q - 2)
                    for k in range(KD):
                        ins = te.matmul(c.pslin[q % 3][:, 0:8],
                                        xn[:, k, q * 128:(q + 1) * 128],
                                        rwb[:, k, :], start=(k == 0),
                                        stop=(k == KD - 1))
                    c.inc(ins, "rt_mm", 1)
                for q in range(NQ):
                    te.wait_ge(c.sem("rt_dn"), base_dn + q + 1)
                    if q >= 2:
                        te.wait_ge(c.sem("rt_w8"), base_w8 + q - 1)
                    ins = te.transpose([c.ps_bA, c.ps_bB][q % 2][0:8, 0:128],
                                       c.nrm_tmp[:, q * 8:q * 8 + 8], c.ident_f[:])
                    c.inc(ins, "rt_tr", 1)

            @blk.scalar
            def _(sc):
                for q in range(NQ):
                    sc.wait_ge(c.sem("rt_cp"), base_cp + q + 1)
                    ins = sc.activation(lg[:, :], wtok[:, :], AF.Sigmoid)
                    c.inc(ins, "rt_sg", 1)

            @blk.vector
            def _(v):
                for q in range(NQ):
                    v.wait_ge(c.sem("rt_mm"), base_mm + q + 1)
                    if q >= 1:
                        v.wait_ge(c.sem("rt_sg"), base_sg + q)
                    ins = v.tensor_copy(wtok[:, :], c.pslin[q % 3][:, 0:8])
                    c.inc(ins, "rt_cp", 1)
                    if upto < 2:
                        v.tensor_copy(yb[:, 6, q * 8:q * 8 + 8],
                                      c.pslin[q % 3][:, 0:8])
                        v.tensor_copy(yb[:, 7, q * 8:q * 8 + 8], wtok[:, :])
                    v.wait_ge(c.sem("rt_sg"), base_sg + q + 1)
                    if upto < 2:
                        v.tensor_copy(yb[:, 2, q * 8:q * 8 + 8], lg[:, :])
                    ins = v.tensor_tensor(msk[:, :], lg[:, :], ebt[:, :], ALU.add)
                    c.inc(ins, "vss"); v.wait_ge(c.sem("vss"), c.cnt["vss"])
                    ins = v.reduce_max(m12[:, 0:1], msk[:, :], axis=AX.X)
                    c.inc(ins, "vss"); v.wait_ge(c.sem("vss"), c.cnt["vss"])
                    ins = v.tensor_single_scalar(wr8[:, :], msk[:, :], m12[:, 0:1],
                                                 ALU.is_ge)
                    c.inc(ins, "vss"); v.wait_ge(c.sem("vss"), c.cnt["vss"])
                    ins = v.scalar_tensor_tensor(msk[:, :], wr8[:, :], -1e30,
                                                 msk[:, :], ALU.mult, ALU.add)
                    c.inc(ins, "vss"); v.wait_ge(c.sem("vss"), c.cnt["vss"])
                    ins = v.reduce_max(m12[:, 1:2], msk[:, :], axis=AX.X)
                    c.inc(ins, "vss"); v.wait_ge(c.sem("vss"), c.cnt["vss"])
                    ins = v.tensor_single_scalar(msk[:, :], msk[:, :], m12[:, 1:2],
                                                 ALU.is_ge)
                    c.inc(ins, "vss"); v.wait_ge(c.sem("vss"), c.cnt["vss"])
                    ins = v.tensor_tensor(msk[:, :], msk[:, :], wr8[:, :], ALU.add)
                    c.inc(ins, "vss"); v.wait_ge(c.sem("vss"), c.cnt["vss"])
                    ins = v.tensor_tensor(wr8[:, :], msk[:, :], lg[:, :], ALU.mult)
                    c.inc(ins, "vss"); v.wait_ge(c.sem("vss"), c.cnt["vss"])
                    ins = v.reduce_sum(m12[:, 0:1], wr8[:, :], axis=AX.X)
                    c.inc(ins, "vss"); v.wait_ge(c.sem("vss"), c.cnt["vss"])
                    v.tensor_single_scalar(m12[:, 0:1], m12[:, 0:1], 1e-8, ALU.add)
                    ins = v.reciprocal(m12[:, 0:1], m12[:, 0:1])
                    c.inc(ins, "vss"); v.wait_ge(c.sem("vss"), c.cnt["vss"])
                    ins = v.tensor_scalar(c.nrm_tmp[:, q * 8:q * 8 + 8], wr8[:, :],
                                          m12[:, 0:1], 0.125, ALU.mult, ALU.mult)
                    c.inc(ins, "rt_dn", 1)
                    if upto < 2:
                        v.tensor_copy(yb[:, 3, q * 8:q * 8 + 8], msk[:, :])
                        v.tensor_copy(yb[:, 4, q * 8:q * 8 + 8], wr8[:, :])
                for q in range(NQ):
                    v.wait_ge(c.sem("rt_tr"), base_tr + q + 1)
                    ins = v.tensor_copy(w8[:, q * 128:(q + 1) * 128],
                                        [c.ps_bA, c.ps_bB][q % 2][0:8, 0:128])
                    c.inc(ins, "rt_w8", 1)
        if upto < 2:
            with nc.Block() as blk:
                @blk.vector
                def _(v):
                    for k in range(5, KD):
                        v.memset(yb[:, k, 0:NT], 0.0)
                    v.memset(yb[:, 0, 0:NT], 0.0)
                    v.tensor_copy(yb[0:8, 0, 0:NT], w8[:, 0:NT])
                    for k in range(KD):
                        v.tensor_copy(yb[:, 5, k * 8:(k + 1) * 8], rwb[:, k, :])
                    v.tensor_copy(yb[:, 12, 0:NT], xn[:, 0, 0:NT])
                    v.tensor_copy(yb[:, 1, 0:8], c.nrm_tmp[:, 0:64].rearrange("p (q e) -> p q e", e=8)[:, 0, :]) if False else None
                    for q in range(4):
                        v.tensor_copy(yb[:, 1, q * 8:q * 8 + 8],
                                      c.nrm_tmp[:, q * 8:q * 8 + 8])
            sandwich_store(c, yb, xf, xout, t0) if False else _store_dbg(c, yb, xout, t0, True)
            continue
        with nc.Block() as blk:
            @blk.tensor
            def _(te):
                te.matmul(c.ps_bA[0:64, 0:NT], e8b[:, :], w8[:, 0:NT],
                          start=True, stop=True)
        with nc.Block() as blk:
            @blk.vector
            def _(v):
                v.tensor_copy(wexp[0:64, 0:NT], c.ps_bA[0:64, 0:NT])
                v.memset(wexp[64:128, 0:NT], 0.125)

        with nc.Block() as blk:
            @blk.tensor
            def _(te):
                for k in range(KD):
                    te.matmul(c.ps_bA[0:72, 0:NT], acgb[:, k, :], xn[:, k, 0:NT],
                              start=(k == 0), stop=(k == KD - 1))
                for k in range(KD):
                    te.matmul(c.ps_bB[0:72, 0:NT], acub[:, k, :], xn[:, k, 0:NT],
                              start=(k == 0), stop=(k == KD - 1))
        with nc.Block() as blk:
            @blk.vector
            def _(v):
                v.tensor_tensor(xawg[0:72, 0:NT], c.ps_bA[0:72, 0:NT],
                                wexp[0:72, 0:NT], ALU.mult)
                v.tensor_tensor(xawu[0:72, 0:NT], c.ps_bB[0:72, 0:NT],
                                wexp[0:72, 0:NT], ALU.mult)

        if upto < 3:
            with nc.Block() as blk:
                @blk.vector
                def _(v):
                    for k in range(KD):
                        v.memset(yb[:, k, 0:NT], 0.0)
                    v.tensor_copy(yb[0:72, 0, 0:NT], wexp[0:72, 0:NT])
                    v.tensor_copy(yb[0:72, 1, 0:NT], xawg[0:72, 0:NT])
                    v.tensor_copy(yb[0:72, 2, 0:NT], xawu[0:72, 0:NT])
            _store_dbg(c, yb, xout, t0, True)
            continue
        def evac_silu(eng, oc, psum):
            return eng.activation(hb[:, oc, 0:NT], psum, AF.Silu)
        emit_linear(c, w_gate, xn, KD, KH, NT, evac_cb=evac_silu,
                    extra=None if DBG_NOEXTRA else (bcgb, xawg, 72))

        def evac_mul(eng, oc, psum):
            return eng.tensor_tensor(hb[:, oc, 0:NT], psum, hb[:, oc, 0:NT], ALU.mult)
        emit_linear(c, w_up, xn, KD, KH, NT, evac_cb=evac_mul,
                    evac_engine="vector", extra=None if DBG_NOEXTRA else (bcub, xawu, 72))

        with nc.Block() as blk:
            @blk.tensor
            def _(te):
                for k in range(KH):
                    te.matmul(c.ps_bA[0:72, 0:NT], acdb[:, k, :], hb[:, k, 0:NT],
                              start=(k == 0), stop=(k == KH - 1))
        with nc.Block() as blk:
            @blk.vector
            def _(v):
                v.tensor_tensor(xawd[0:72, 0:NT], c.ps_bA[0:72, 0:NT],
                                wexp[0:72, 0:NT], ALU.mult)

        if upto < 4:
            with nc.Block() as blk:
                @blk.vector
                def _(v):
                    for k in range(KD):
                        v.tensor_copy(yb[:, k, 0:NT], hb[:, k, 0:NT])
            _store_dbg(c, yb, xout, t0, True)
            continue
        def evac_y(eng, oc, psum):
            return eng.activation(yb[:, oc, 0:NT], psum, AF.Identity)
        emit_linear(c, w_down, hb, KH, KD, NT, evac_cb=evac_y,
                    extra=(bcdb, xawd, 72))

        if upto < 5:
            _store_dbg(c, yb, xout, t0, True)
            continue
        sandwich_store(c, yb, xf, xout, t0)
    return c.close()


def run_l0f(XT, inp, upto=99):
    nc = build_l0f(upto)
    acat_g = np.ascontiguousarray(np.concatenate(
        [inp["L0_ffn_A_gate"].transpose(1, 0, 2).reshape(D, 64),
         inp["L0_ffn_sA_gate"]], axis=1))
    acat_u = np.ascontiguousarray(np.concatenate(
        [inp["L0_ffn_A_up"].transpose(1, 0, 2).reshape(D, 64),
         inp["L0_ffn_sA_up"]], axis=1))
    acat_d = np.ascontiguousarray(np.concatenate(
        [inp["L0_ffn_A_down"].transpose(1, 0, 2).reshape(5504, 64),
         inp["L0_ffn_sA_down"]], axis=1))
    bcat_g = np.ascontiguousarray(np.concatenate(
        [inp["L0_ffn_B_gate"].reshape(64, 5504), inp["L0_ffn_sB_gate"]], axis=0))
    bcat_u = np.ascontiguousarray(np.concatenate(
        [inp["L0_ffn_B_up"].reshape(64, 5504), inp["L0_ffn_sB_up"]], axis=0))
    bcat_d = np.ascontiguousarray(np.concatenate(
        [inp["L0_ffn_B_down"].reshape(64, D), inp["L0_ffn_sB_down"]], axis=0))
    e8 = np.zeros((8, 64), np.float32)
    for e in range(8):
        e8[e, e * 8:(e + 1) * 8] = 1.0
    ebias = np.ascontiguousarray(
        np.broadcast_to(inp["L0_ffn_expert_bias"], (128, 8)).astype(np.float32))
    in_maps = []
    for cid in range(NCORES):
        t0 = cid * TPC
        in_maps.append({
            "xin": np.ascontiguousarray(XT[:, t0:t0 + TPC]),
            "w_gate": inp["L0_ffn_W_gate"], "w_up": inp["L0_ffn_W_up"],
            "w_down": inp["L0_ffn_W_down"], "w_rt": inp["L0_ffn_W_router"],
            "acat_g": acat_g, "acat_u": acat_u, "acat_d": acat_d,
            "bcat_g": bcat_g, "bcat_u": bcat_u, "bcat_d": bcat_d,
            "ebias": ebias, "e8": e8,
        })
    res = run_bass_kernel_spmd(nc, in_maps, core_ids=list(range(NCORES)))
    if upto < 2:
        np.save("/root/problem/scratch/dbg_rw.npy", res.results[0]["dbg_rw"])
    return np.concatenate([res.results[i]["xout"] for i in range(NCORES)], axis=1)


def _emit_dequant(c, packed, scalesT, OSL, HIN, out_dram):
    nc = c.nc
    CC = HIN // 8
    G = HIN // 128
    KH = HIN // 128
    pk = c.sb(f"pk_{OSL}", [128, CC], I32)
    sh = c.sb(f"sh_{OSL}", [128, CC], I32)
    m7 = c.sb(f"m7_{OSL}", [128, CC], I32)
    s8 = c.sb(f"s8_{OSL}", [128, CC], I32)
    mf = c.sb(f"mf_{OSL}", [128, CC], F32)
    r1 = c.sb(f"r1_{OSL}", [128, CC], F32)
    r2 = c.sb(f"r2_{OSL}", [128, CC], F32)
    sg = c.sb(f"sg_{OSL}", [128, CC], F32)
    vt = c.sb(f"vt_{OSL}", [128, HIN], F32)
    sc2 = c.sb(f"sc2_{OSL}", [128, G], F16)
    scf = c.sb(f"scf_{OSL}", [128, G], F32)
    wb = c.sb(f"wb_{OSL}", [128, HIN], BF16)
    wt = c.sb(f"wt_{OSL}", [128, KH, 128], BF16)
    pst = [c.ps(f"pst{i}_{OSL}", [128, 128], BF16) for i in range(2)]
    NOC = OSL // 128
    for oc in range(NOC):
        with nc.Block() as blk:
            @blk.sync
            def _(sy):
                c.inc(sy.dma_start(pk[:, :], packed[oc * 128:(oc + 1) * 128, :]),
                      "dq_ld", 16)
                c.inc(sy.dma_start(sc2[:, :], scalesT[oc * 128:(oc + 1) * 128, :]),
                      "dq_ld", 16)

            @blk.vector
            def _(v):
                v.wait_ge(c.sem("dq_ld"), c.cnt["dq_ld"])
                v.tensor_copy(scf[:, :], sc2[:, :])
        for i in range(8):
            with nc.Block() as blk:
                @blk.vector
                def _(v):
                    if i == 0:
                        v.tensor_copy(sh[:, :], pk[:, :])
                    else:
                        v.tensor_single_scalar(sh[:, :], pk[:, :], 4 * i,
                                               ALU.logical_shift_right)
                    v.tensor_single_scalar(m7[:, :], sh[:, :], 7, ALU.bitwise_and)
                    v.tensor_single_scalar(s8[:, :], sh[:, :], 8, ALU.bitwise_and)

                @blk.scalar
                def _(sc):
                    pass
            with nc.Block() as blk:
                @blk.scalar
                def _(sc):
                    sc.activation(mf[:, :], m7[:, :], AF.Identity)

                @blk.vector
                def _(v):
                    v.tensor_scalar(sg[:, :], s8[:, :], -0.25, 1.0, ALU.mult, ALU.add)
            with nc.Block() as blk:
                @blk.vector
                def _(v):
                    v.tensor_scalar(r1[:, :], mf[:, :], 4.0, 0.0, ALU.subtract, ALU.max)
                    v.tensor_scalar(r2[:, :], mf[:, :], 6.0, 0.0, ALU.subtract, ALU.max)
            with nc.Block() as blk:
                @blk.vector
                def _(v):
                    v.tensor_tensor(r1[:, :], r1[:, :], mf[:, :], ALU.add)
                    v.scalar_tensor_tensor(r2[:, :], r2[:, :], 2.0, r1[:, :],
                                           ALU.mult, ALU.add)
            with nc.Block() as blk:
                @blk.vector
                def _(v):
                    v.tensor_single_scalar(sg[:, :], sg[:, :], 0.5, ALU.mult)
                    v.tensor_tensor(vt[:, i::8], r2[:, :], sg[:, :], ALU.mult)
        with nc.Block() as blk:
            @blk.vector
            def _(v):
                for g in range(G):
                    v.tensor_single_scalar(wb[:, g * 128:(g + 1) * 128],
                                           vt[:, g * 128:(g + 1) * 128],
                                           scf[:, g:g + 1], ALU.mult)
        with nc.Block() as blk:
            base_tr = c.cnt.get("dq_tr", 0)
            base_ev = c.cnt.get("dq_ev", 0)

            @blk.tensor
            def _(te):
                for k in range(KH):
                    if k >= 2:
                        te.wait_ge(c.sem("dq_ev"), base_ev + k - 1)
                    ins = te.transpose(pst[k % 2][:, :], wb[:, k * 128:(k + 1) * 128],
                                       c.ident_bf[:])
                    c.inc(ins, "dq_tr", 1)

            @blk.vector
            def _(v):
                for k in range(KH):
                    v.wait_ge(c.sem("dq_tr"), base_tr + k + 1)
                    ins = v.tensor_copy(wt[:, k, :], pst[k % 2][:, :])
                    c.inc(ins, "dq_ev", 1)
        with nc.Block() as blk:
            @blk.gpsimd
            def _(g):
                dst = out_dram[:, oc * 128:(oc + 1) * 128]
                ins = g.dma_start(dst.rearrange("(k p) o -> p k o", p=128),
                                  wt[:, :, :])
                c.inc(ins, "dq_st", 16)
                g.wait_ge(c.sem("dq_st"), c.cnt["dq_st"])


def build_l1dq():
    c = C("l1dq")
    gup_p = c.din("gup_p", [1408, 256], I32)
    gup_st = c.din("gup_st", [1408, 16], F16)
    dwn_p = c.din("dwn_p", [256, 704], I32)
    dwn_st = c.din("dwn_st", [256, 44], F16)
    w_guT = c.dout("w_guT", [2048, 1408], BF16)
    w_dT = c.dout("w_dT", [5632, 256], BF16)
    make_consts(c)
    _emit_dequant(c, gup_p, gup_st, 1408, 2048, w_guT)
    _emit_dequant(c, dwn_p, dwn_st, 256, 5632, w_dT)
    return c.close()


def run_l1dq(inp):
    nc = build_l1dq()
    in_maps = []
    gs = np.ascontiguousarray(inp["L1_ffn_gate_up_scales"].T)
    ds = np.ascontiguousarray(inp["L1_ffn_down_scales"].T)
    for cid in range(NCORES):
        in_maps.append({
            "gup_p": np.ascontiguousarray(
                inp["L1_ffn_gate_up_packed"][cid * 1408:(cid + 1) * 1408]),
            "gup_st": np.ascontiguousarray(gs[cid * 1408:(cid + 1) * 1408]),
            "dwn_p": np.ascontiguousarray(
                inp["L1_ffn_down_packed"][cid * 256:(cid + 1) * 256]),
            "dwn_st": np.ascontiguousarray(ds[cid * 256:(cid + 1) * 256]),
        })
    res = run_bass_kernel_spmd(nc, in_maps, core_ids=list(range(NCORES)))
    w_guT = np.concatenate([res.results[i]["w_guT"] for i in range(NCORES)], axis=1)
    w_dT = np.concatenate([res.results[i]["w_dT"] for i in range(NCORES)], axis=1)
    return w_guT, w_dT


def build_l1f():
    c = C("l1f")
    nc = c.nc
    KH = 44
    xin = c.din("xin", [D, TPC])
    w_guT = c.din("w_guT", [D, 11264], BF16)
    w_dT = c.din("w_dT", [5632, D], BF16)
    xout = c.dout("xout", [D, TPC])

    make_consts(c)
    make_lin_pool(c, kin_max=16, npsum=4)
    make_norm_pool(c)

    xf = c.sb("xf", [128, KD, NT], F32)
    xn = c.sb("xn", [128, KD, NT], BF16)
    hb = c.sb("hb", [128, KH, NT], BF16)
    yb = c.sb("yb", [128, KD, NT], F32)
    zb = xn

    for tt in range(TPC // NT):
        t0 = tt * NT
        with nc.Block() as blk:
            @blk.sync
            def _(sy):
                dma_in_act(c, sy, xf, xin, t0, NT)

            @blk.vector
            def _(v):
                v.wait_ge(c.sem("xld"), c.cnt["xld"])
        emit_norm(c, xf, 0, NT, KD, kind="ln", eps=1e-5, dst3=xn)

        def evac_silu(eng, oc, psum):
            return eng.activation(hb[:, oc, 0:NT], psum, AF.Silu)
        emit_linear(c, w_guT, xn, KD, KH, NT, evac_cb=evac_silu, w_bf16=True)

        def evac_mul(eng, oc, psum):
            return eng.tensor_tensor(hb[:, oc, 0:NT], psum, hb[:, oc, 0:NT], ALU.mult)
        emit_linear(c, w_guT, xn, KD, KH, NT, OC0=KH, evac_cb=evac_mul,
                    evac_engine="vector", w_bf16=True)

        def evac_y(eng, oc, psum):
            return eng.activation(yb[:, oc, 0:NT], psum, AF.Identity)
        emit_linear(c, w_dT, hb, KH, KD, NT, evac_cb=evac_y, w_bf16=True)

        emit_norm(c, yb, 0, NT, KD, kind="rms", eps=1e-6, dst3=zb)
        sandwich_store(c, zb, xf, xout, t0)
    return c.close()


def run_l1f(XT, inp, w_guT, w_dT):
    nc = build_l1f()
    in_maps = []
    for cid in range(NCORES):
        t0 = cid * TPC
        in_maps.append({
            "xin": np.ascontiguousarray(XT[:, t0:t0 + TPC]),
            "w_guT": w_guT, "w_dT": w_dT,
        })
    res = run_bass_kernel_spmd(nc, in_maps, core_ids=list(range(NCORES)))
    return np.concatenate([res.results[i]["xout"] for i in range(NCORES)], axis=1)


def make_attn_pool(c, dv=128):
    c.at_pt = [c.sb(f"at_pt{i}", [128, NT], BF16) for i in range(3)]
    c.at_msk = c.sb("at_msk", [128, 4, NT], BF16)
    c.at_row = c.sb("at_row", [1, NT], F32)
    c.at_bc = c.sb("at_bc", [128, NT], F32)
    c.at_psS = [c.pslin[0], c.pslin[1]]
    c.at_den = c.ps_rowA
    c.at_ctx = c.ps_bA
    c.at_psb = c.ps_bB
    with c.nc.Block() as blk:
        @blk.vector
        def _(v):
            for r in range(4):
                v.tensor_scalar(c.at_msk[:, r, :], c.iot_j[:, 0:NT], float(r * 128),
                                c.iot_p[:, 0:1], ALU.subtract, ALU.is_ge)


def emit_attn_core(c, NH, scale, score_mm, pv_lhsT, ctx_store, dv=128,
                   pv_base=None, ctx_dma=None):
    nc = c.nc
    NQT = S // NT
    jobs = []
    for h in range(NH):
        for qt in range(NQT):
            jobs.append((h, qt, 4 * qt + 4))
    with nc.Block() as blk:
        b_mm = c.cnt.get("at_mm", 0)
        b_ex = c.cnt.get("at_ex", 0)
        b_mk = c.cnt.get("at_mk", 0)
        b_pv = c.cnt.get("at_pv", 0)
        b_rc = c.cnt.get("at_rc", 0)
        b_bc = c.cnt.get("at_bc", 0)
        b_cx = c.cnt.get("at_cx", 0)
        b_cd = c.cnt.get("at_cd", 0)
        cmm = [b_mm]; cex = [b_ex]; cmk = [b_mk]; cpv = [b_pv]
        for (h, qt, nkc) in jobs:
            cmm.append(cmm[-1] + nkc)
            cex.append(cex[-1] + nkc)
            cmk.append(cmk[-1] + 4)
            cpv.append(cpv[-1] + nkc)

        @blk.tensor
        def _(te):
            for j, (h, qt, nkc) in enumerate(jobs):
                d0 = 4 * qt
                for kc in range(nkc):
                    wv_ = cex[j] + max(kc - 1, 0)
                    if wv_ > b_ex and not (j == 0 and kc < 2):
                        te.wait_ge(c.sem("at_ex"), wv_)
                    ins = score_mm(te, h, kc, qt, c.at_psS[kc % 2][:, 0:NT])
                    c.inc(ins, "at_mm", 1)
                    if kc >= d0:
                        te.wait_ge(c.sem("at_mk"), cmk[j] + (kc - d0) + 1)
                    else:
                        te.wait_ge(c.sem("at_ex"), cex[j] + kc + 1)
                    if kc == 0 and j > 0:
                        te.wait_ge(c.sem("at_cx"), b_cx + j)
                    te.matmul(c.at_den[:, 0:NT], c.ones_bf[:], c.at_pt[kc % 3][:, 0:NT],
                              start=(kc == 0), stop=(kc == nkc - 1))
                    b0 = pv_base(h) if pv_base else 0
                    ins = te.matmul(c.at_ctx[b0:b0 + dv, 0:NT], pv_lhsT(h, kc),
                                    c.at_pt[kc % 3][:, 0:NT],
                                    start=(kc == 0), stop=(kc == nkc - 1),
                                    tile_position=(0, b0) if b0 else None)
                    c.inc(ins, "at_pv", 1)
                te.wait_ge(c.sem("at_rc"), b_rc + j + 1)
                ins = te.matmul(c.at_psb[:, 0:NT], c.ones_f1[:], c.at_row[:, 0:NT],
                                start=True, stop=True)
                c.inc(ins, "at_bc", 1)

        if ctx_dma is not None:
            @blk.gpsimd
            def _(g):
                for j, (h, qt, nkc) in enumerate(jobs):
                    g.wait_ge(c.sem("at_cx"), b_cx + j + 1)
                    ins = ctx_dma(g, j, h, qt)
                    c.inc(ins, "at_cd", 16)
                g.wait_ge(c.sem("at_cd"), c.cnt["at_cd"])

        @blk.scalar
        def _(sc):
            for j, (h, qt, nkc) in enumerate(jobs):
                for kc in range(nkc):
                    sc.wait_ge(c.sem("at_mm"), cmm[j] + kc + 1)
                    wv_ = cpv[j] + max(kc - 2, 0)
                    if wv_ > b_pv:
                        sc.wait_ge(c.sem("at_pv"), wv_)
                    ins = sc.activation(c.at_pt[kc % 3][:, 0:NT],
                                        c.at_psS[kc % 2][:, 0:NT], AF.Exp,
                                        scale=scale)
                    c.inc(ins, "at_ex", 1)

        @blk.vector
        def _(v):
            for j, (h, qt, nkc) in enumerate(jobs):
                d0 = 4 * qt
                for kc in range(d0, nkc):
                    v.wait_ge(c.sem("at_ex"), cex[j] + kc + 1)
                    ins = v.tensor_tensor(c.at_pt[kc % 3][:, 0:NT],
                                          c.at_pt[kc % 3][:, 0:NT],
                                          c.at_msk[:, kc - d0, 0:NT], ALU.mult)
                    c.inc(ins, "at_mk", 1)
                v.wait_ge(c.sem("at_pv"), cpv[j + 1])
                ins = v.tensor_copy(c.at_row[:, 0:NT], c.at_den[:, 0:NT])
                c.inc(ins, "vss"); v.wait_ge(c.sem("vss"), c.cnt["vss"])
                ins = v.reciprocal(c.at_row[:, 0:NT], c.at_row[:, 0:NT])
                c.inc(ins, "at_rc", 1)
                v.wait_ge(c.sem("at_bc"), b_bc + j + 1)
                v.tensor_copy(c.at_bc[:, 0:NT], c.at_psb[:, 0:NT])
                b0 = pv_base(h) if pv_base else 0
                if ctx_dma is not None and j >= 2:
                    v.wait_ge(c.sem("at_cd"), b_cd + 16 * (j - 1))
                ins = ctx_store(v, h, qt, c.at_ctx[b0:b0 + dv, 0:NT],
                                c.at_bc[b0:b0 + dv, 0:NT])
                c.inc(ins, "at_cx", 1)


def build_l1a():
    c = C("l1a")
    nc = c.nc
    xin = c.din("xin", [D, S])
    wq = c.din("wq", [D, 512])
    wk = c.din("wk", [D, 128])
    wv = c.din("wv", [D, 128])
    bq = c.din("bq", [128, 4])
    bk = c.din("bk", [128, 1])
    bv = c.din("bv", [128, 1])
    ctx_out = c.dout("ctx", [512, S], BF16)

    make_consts(c)
    make_lin_pool(c, kin_max=16, npsum=4)
    make_norm_pool(c)
    make_attn_pool(c)

    xf = c.sb("xf", [128, KD, NT], F32)
    xn = c.sb("xn", [128, KD, NT], BF16)
    q3 = c.sb("q3", [128, 4, S], BF16)
    k3 = c.sb("k3", [128, 1, S], BF16)
    vtok = c.sb("vtok", [128, 32, 128], BF16)
    vst = c.sb("vst", [128, NT], F32)
    cxs = c.sb("cxs", [128, 4, S], BF16)
    bq2 = c.sb("bq2", [128, 4], F32)
    bk2 = c.sb("bk2", [128, 1], F32)
    bv2 = c.sb("bv2", [128, 1], F32)
    pst = [c.pslin[2][:, 0:128], c.pslin[3][:, 0:128]]

    with nc.Block() as blk:
        @blk.sync
        def _(sy):
            c.inc(sy.dma_start(bq2[:, :], bq[:, :]), "init", 16)
            c.inc(sy.dma_start(bk2[:, :], bk[:, :]), "init", 16)
            c.inc(sy.dma_start(bv2[:, :], bv[:, :]), "init", 16)

        @blk.vector
        def _(v):
            v.wait_ge(c.sem("init"), c.cnt["init"])

    for tt in range(S // NT):
        t0 = tt * NT
        with nc.Block() as blk:
            @blk.sync
            def _(sy):
                dma_in_act(c, sy, xf, xin, t0, NT)

            @blk.vector
            def _(v):
                v.wait_ge(c.sem("xld"), c.cnt["xld"])
        emit_norm(c, xf, 0, NT, KD, kind="ln", eps=1e-5, dst3=xn)
        emit_linear(c, wq, xn, KD, 4, NT, dst3=q3, dtt0=0, dst_t0=t0, bias2=bq2)
        emit_linear(c, wk, xn, KD, 1, NT, dst3=k3,
                    dtt0=0, dst_t0=t0, bias2=bk2)
        def evac_v(eng, oc, psum):
            return eng.activation(vst[:, 0:NT], psum, AF.Identity, bias=bv2[:, 0:1])
        emit_linear(c, wv, xn, KD, 1, NT, evac_cb=evac_v)
        with nc.Block() as blk:
            b_tr = c.cnt.get("vtr", 0)
            b_ev = c.cnt.get("vev", 0)

            @blk.tensor
            def _(te):
                for s4 in range(4):
                    if s4 >= 2:
                        te.wait_ge(c.sem("vev"), b_ev + s4 - 1)
                    ins = te.transpose(pst[s4 % 2],
                                       vst[:, s4 * 128:(s4 + 1) * 128], c.ident_f[:])
                    c.inc(ins, "vtr", 1)

            @blk.vector
            def _(v):
                for s4 in range(4):
                    v.wait_ge(c.sem("vtr"), b_tr + s4 + 1)
                    ins = v.tensor_copy(vtok[:, tt * 4 + s4, :], pst[s4 % 2])
                    c.inc(ins, "vev", 1)

    def score_mm(te, h, kc, qt, ps):
        return te.matmul(ps, k3[:, 0, kc * 128:(kc + 1) * 128],
                         q3[:, h, qt * NT:(qt + 1) * NT], start=True, stop=True)

    def pv_lhsT(h, kc):
        return vtok[:, kc, :]

    def ctx_store(v, h, qt, ps_ctx, bc):
        tmp = c.nrm_tmp[:, 0:NT]
        v.tensor_tensor(tmp, ps_ctx, bc, ALU.mult)
        return v.tensor_copy(cxs[:, h, qt * NT:(qt + 1) * NT], tmp)

    emit_attn_core(c, 4, 1.0 / float(np.sqrt(128.0)), score_mm, pv_lhsT, ctx_store)

    with nc.Block() as blk:
        @blk.gpsimd
        def _(g):
            ins = g.dma_start(ctx_out.rearrange("(h p) t -> p h t", p=128),
                              cxs[:, :, :])
            c.inc(ins, "xst", 16)
            g.wait_ge(c.sem("xst"), c.cnt["xst"])
    return c.close()


def build_a2(name, cin_rows=D):
    c = C(name)
    nc = c.nc
    KC = cin_rows // 128
    xin = c.din("xin", [D, TPC])
    cin = c.din("cin", [cin_rows, TPC], BF16)
    wo = c.din("wo", [cin_rows, D])
    xout = c.dout("xout", [D, TPC])
    make_consts(c)
    make_lin_pool(c, kin_max=16, npsum=4)
    make_norm_pool(c)
    xf = c.sb("xf", [128, KD, NT], F32)
    cb = c.sb("cb", [128, KC, NT], BF16)
    yb = c.sb("yb", [128, KD, NT], F32)
    for tt in range(TPC // NT):
        t0 = tt * NT
        with nc.Block() as blk:
            @blk.sync
            def _(sy):
                dma_in_act(c, sy, xf, xin, t0, NT)
                ins = sy.dma_start(cb[:, :, 0:NT],
                                   cin[:, t0:t0 + NT].rearrange(
                                       "(k p) t -> p k t", p=128))
                c.inc(ins, "xld", 16)

            @blk.vector
            def _(v):
                v.wait_ge(c.sem("xld"), c.cnt["xld"])

        def evac_y(eng, oc, psum):
            return eng.activation(yb[:, oc, 0:NT], psum, AF.Identity)
        emit_linear(c, wo, cb, KC, KD, NT, evac_cb=evac_y)
        sandwich_store(c, yb, xf, xout, t0)
    return c.close()


def run_attn_a2(name, XT, ctxT, wo_io):
    nc = build_a2(name, cin_rows=ctxT.shape[0])
    in_maps = []
    for cid in range(NCORES):
        t0 = cid * TPC
        in_maps.append({
            "xin": np.ascontiguousarray(XT[:, t0:t0 + TPC]),
            "cin": np.ascontiguousarray(ctxT[:, t0:t0 + TPC]),
            "wo": wo_io,
        })
    res = run_bass_kernel_spmd(nc, in_maps, core_ids=list(range(NCORES)))
    return np.concatenate([res.results[i]["xout"] for i in range(NCORES)], axis=1)


def run_l1a(XT, inp):
    nc = build_l1a()
    in_maps = []
    for cid in range(NCORES):
        b, j = cid // 4, cid % 4
        in_maps.append({
            "xin": np.ascontiguousarray(XT[:, b * S:(b + 1) * S]),
            "wq": np.ascontiguousarray(inp["L1_attn_wq"][:, j * 512:(j + 1) * 512]),
            "wk": np.ascontiguousarray(inp["L1_attn_wk"][:, j * 128:(j + 1) * 128]),
            "wv": np.ascontiguousarray(inp["L1_attn_wv"][:, j * 128:(j + 1) * 128]),
            "bq": _ppart(inp["L1_attn_bq"][j * 512:(j + 1) * 512]),
            "bk": _ppart(inp["L1_attn_bk"][j * 128:(j + 1) * 128]),
            "bv": _ppart(inp["L1_attn_bv"][j * 128:(j + 1) * 128]),
        })
    res = run_bass_kernel_spmd(nc, in_maps, core_ids=list(range(NCORES)))
    ctxT = np.zeros((D, NTOK), np.float32).astype(None)
    import ml_dtypes
    ctxT = np.zeros((D, NTOK), dtype=ml_dtypes.bfloat16)
    for cid in range(NCORES):
        b, j = cid // 4, cid % 4
        ctxT[j * 512:(j + 1) * 512, b * S:(b + 1) * S] = res.results[cid]["ctx"]
    return ctxT


def build_l0a(dbg=False):
    c = C("l0a")
    nc = c.nc
    xin = c.din("xin", [D, S])
    wq = c.din("wq", [D, 512])
    wkv = c.din("wkv", [D, 128])
    cos2 = c.din("cos2", [128, S])
    ssin = c.din("ssin", [128, S])
    ctx_out = c.dout("ctx", [512, S], BF16)

    make_consts(c)
    make_lin_pool(c, kin_max=16, npsum=4)
    make_norm_pool(c)
    make_attn_pool(c)

    xf = c.sb("xf", [128, KD, NT], F32)
    xn = c.sb("xn", [128, KD, NT], BF16)
    q3 = c.sb("q3", [128, 4, S], BF16)
    k3 = c.sb("k3", [128, 1, S], BF16)
    vtok = c.sb("vtok", [128, 32, 64], BF16)
    qraw = c.sb("qraw", [128, 4, NT], F32)
    kvraw = c.sb("kvraw", [128, 1, NT], F32)
    rot = c.sb("rot", [128, NT], F32)
    rotb = c.sb("rotb", [128, NT], F32)
    rt2 = c.sb("rt2", [128, NT], F32)
    cosb = c.sb("cosb", [128, S], BF16)
    sinb = c.sb("sinb", [128, S], BF16)
    cxs = c.sb("cxs", [128, 4, S], BF16)
    pst = [c.pslin[2][:, 0:128], c.pslin[3][:, 0:128]]

    with nc.Block() as blk:
        @blk.gpsimd
        def _(g):
            c.inc(g.dma_start(cosb[:, :], cos2[:, :]), "init", 16)
            c.inc(g.dma_start(sinb[:, :], ssin[:, :]), "init", 16)
            g.wait_ge(c.sem("init"), c.cnt["init"])

    for tt in range(S // NT):
        t0 = tt * NT
        with nc.Block() as blk:
            @blk.sync
            def _(sy):
                dma_in_act(c, sy, xf, xin, t0, NT)

            @blk.vector
            def _(v):
                v.wait_ge(c.sem("xld"), c.cnt["xld"])
        emit_norm(c, xf, 0, NT, KD, kind="ln", eps=1e-5, dst3=xn)

        def evac_q(eng, oc, psum):
            return eng.activation(qraw[:, oc, 0:NT], psum, AF.Identity)
        emit_linear(c, wq, xn, KD, 4, NT, evac_cb=evac_q)

        def evac_kv(eng, oc, psum):
            return eng.activation(kvraw[:, 0, 0:NT], psum, AF.Identity)
        emit_linear(c, wkv, xn, KD, 1, NT, evac_cb=evac_kv)

        with nc.Block() as blk:
            base_sw = c.cnt.get("rp_sw", 0)
            base_vq = c.cnt.get("rp_vq", 0)

            @blk.gpsimd
            def _(g):
                for ch in range(4):
                    buf = [rot, rotb][ch % 2]
                    if ch >= 2:
                        g.wait_ge(c.sem("rp_vq"), base_vq + ch - 1)
                    for half in range(4):
                        b0 = half * 32
                        b1 = (half ^ 1) * 32
                        ins = g.dma_start(buf[b0:b0 + 32, 0:NT],
                                          qraw[b1:b1 + 32, ch, 0:NT])
                        c.inc(ins, "rp_sw", 16)
                for half in range(2):
                    b0 = half * 32
                    b1 = (half ^ 1) * 32
                    ins = g.dma_start(rt2[b0:b0 + 32, 0:NT],
                                      kvraw[b1:b1 + 32, 0, 0:NT])
                    c.inc(ins, "rp_sw", 16)

            @blk.vector
            def _(v):
                for ch in range(4):
                    buf = [rot, rotb][ch % 2]
                    v.wait_ge(c.sem("rp_sw"), base_sw + 16 * 4 * (ch + 1))
                    v.tensor_tensor(buf[:, 0:NT], buf[:, 0:NT],
                                    sinb[:, t0:t0 + NT], ALU.mult)
                    v.tensor_tensor(c.nrm_tmp[:, 0:NT], qraw[:, ch, 0:NT],
                                    cosb[:, t0:t0 + NT], ALU.mult)
                    ins = v.tensor_tensor(q3[:, ch, t0:t0 + NT], c.nrm_tmp[:, 0:NT],
                                          buf[:, 0:NT], ALU.add)
                    c.inc(ins, "rp_vq", 1)
                v.wait_ge(c.sem("rp_sw"), base_sw + 16 * 18)
                v.tensor_tensor(rt2[0:64, 0:NT], rt2[0:64, 0:NT],
                                sinb[0:64, t0:t0 + NT], ALU.mult)
                v.tensor_tensor(c.nrm_tmp[0:64, 0:NT], kvraw[0:64, 0, 0:NT],
                                cosb[0:64, t0:t0 + NT], ALU.mult)
                ins = v.tensor_tensor(k3[0:64, 0, t0:t0 + NT],
                                      c.nrm_tmp[0:64, 0:NT],
                                      rt2[0:64, 0:NT], ALU.add)
                c.inc(ins, "rp_k", 1)
        with nc.Block() as blk:
            @blk.gpsimd
            def _(g):
                c.inc(g.dma_start(k3[64:128, 0, t0:t0 + NT],
                                  k3[0:64, 0, t0:t0 + NT]), "kdup", 16)
                g.wait_ge(c.sem("kdup"), c.cnt["kdup"])
        with nc.Block() as blk:
            b_tr = c.cnt.get("vtr", 0)
            b_ev = c.cnt.get("vev", 0)

            @blk.tensor
            def _(te):
                for s4 in range(4):
                    if s4 >= 2:
                        te.wait_ge(c.sem("vev"), b_ev + s4 - 1)
                    ins = te.transpose(pst[s4 % 2][0:128, 0:64],
                                       kvraw[64:128, 0, s4 * 128:(s4 + 1) * 128],
                                       c.ident_f[64:128, 64:128])
                    c.inc(ins, "vtr", 1)

            @blk.vector
            def _(v):
                for s4 in range(4):
                    v.wait_ge(c.sem("vtr"), b_tr + s4 + 1)
                    ins = v.tensor_copy(vtok[:, tt * 4 + s4, :], pst[s4 % 2][0:128, 0:64])
                    c.inc(ins, "vev", 1)

    if dbg:
        qdbg = c.dout("qdbg", [128, 4, S], BF16)
        kdbg = c.dout("kdbg", [128, 1, S], BF16)
        with nc.Block() as blk:
            @blk.gpsimd
            def _(g):
                c.inc(g.dma_start(qdbg[:, :, :], q3[:, :, :]), "dbq", 16)
                c.inc(g.dma_start(kdbg[:, :, :], k3[:, :, :]), "dbq", 16)
                g.wait_ge(c.sem("dbq"), c.cnt["dbq"])

    def score_mm(te, h, kc, qt, ps):
        b0 = (h % 2) * 64
        return te.matmul(ps, k3[b0:b0 + 64, 0, kc * 128:(kc + 1) * 128],
                         q3[b0:b0 + 64, h // 2,
                            qt * NT:(qt + 1) * NT], start=True, stop=True)

    def pv_lhsT(h, kc):
        return vtok[:, kc, :]

    def pv_base(h):
        return (h % 2) * 64

    def ctx_store(v, h, qt, ps_ctx, bc):
        b0 = (h % 2) * 64
        tmp = c.nrm_tmp[b0:b0 + 64, 0:NT]
        v.tensor_tensor(tmp, ps_ctx, bc, ALU.mult)
        return v.tensor_copy(cxs[b0:b0 + 64, h // 2, qt * NT:(qt + 1) * NT], tmp)

    emit_attn_core(c, 8, 0.125, score_mm, pv_lhsT, ctx_store, dv=64,
                   pv_base=pv_base)

    with nc.Block() as blk:
        @blk.gpsimd
        def _(g):
            for hl in range(8):
                b0 = (hl % 2) * 64
                ins = g.dma_start(ctx_out[hl * 64:(hl + 1) * 64, :],
                                  cxs[b0:b0 + 64, hl // 2, :])
                c.inc(ins, "xst", 16)
            g.wait_ge(c.sem("xst"), c.cnt["xst"])
    return c.close()


def run_l0a(XT, inp, dbg=False):
    nc = build_l0a(dbg)
    HD = 64
    inv = 1.0 / 10000.0 ** (np.arange(0, HD, 2, dtype=np.float32) / HD)
    freqs = np.arange(S, dtype=np.float32)[:, None] * inv[None, :]
    emb = np.concatenate([freqs, freqs], -1)
    cos1 = np.cos(emb).T
    sin1 = np.sin(emb).T
    ssin1 = sin1.copy()
    ssin1[0:32] = -sin1[0:32]
    cos2 = np.ascontiguousarray(np.concatenate([cos1, cos1], 0).astype(np.float32))
    ssin = np.ascontiguousarray(np.concatenate([ssin1, ssin1], 0).astype(np.float32))
    w_qkv = inp["L0_attn_w_qkv"]
    in_maps = []
    for cid in range(NCORES):
        b, j = cid // 4, cid % 4
        in_maps.append({
            "xin": np.ascontiguousarray(XT[:, b * S:(b + 1) * S]),
            "wq": np.ascontiguousarray(w_qkv[:, j * 512:(j + 1) * 512]),
            "wkv": np.ascontiguousarray(w_qkv[:, 2048:2176]),
            "cos2": cos2, "ssin": ssin,
        })
    res = run_bass_kernel_spmd(nc, in_maps, core_ids=list(range(NCORES)))
    import ml_dtypes
    ctxT = np.zeros((D, NTOK), dtype=ml_dtypes.bfloat16)
    for cid in range(NCORES):
        b, j = cid // 4, cid % 4
        ctxT[j * 512:(j + 1) * 512, b * S:(b + 1) * S] = res.results[cid]["ctx"]
    if dbg:
        np.save("/root/problem/scratch/l0a_q3.npy",
                res.results[0]["qdbg"].astype(np.float32))
        np.save("/root/problem/scratch/l0a_k3.npy",
                res.results[0]["kdbg"].astype(np.float32))
    return ctxT


def build_l2a():
    c = C("l2a")
    nc = c.nc
    xin = c.din("xin", [D, S])
    wqd_t = c.din("wqd_t", [D, 1024])
    qnw = c.din("qnw", [128, 8])
    wqu_t = c.din("wqu_t", [1024, 512])
    wqr_t = c.din("wqr_t", [1024, 256])
    wkvd_t = c.din("wkvd_t", [D, 512])
    kvnw = c.din("kvnw", [128, 4])
    wkvu_t = c.din("wkvu_t", [512, 1024])
    wkr_t = c.din("wkr_t", [D, 128])
    cosi = c.din("cosi", [128, S])
    ssini = c.din("ssini", [128, S])
    ctx_out = c.dout("ctx", [512, S], BF16)

    make_consts(c)
    make_lin_pool(c, kin_max=8, npsum=4)
    make_norm_pool(c)
    make_attn_pool(c)

    xn = c.sb("xn", [128, KD, NT], BF16)
    xf = xn
    qcn = c.sb("qcn", [128, 8, NT], BF16)
    qcb = qcn
    kvcn = c.sb("kvcn", [128, 4, NT], BF16)
    kvcb = kvcn
    qn3 = c.sb("qn3", [128, 4, S], BF16)
    qpe3 = c.sb("qpe3", [128, 2, S], BF16)
    qpb = c.sb("qpb", [128, 2, NT], BF16)
    kn3 = c.sb("kn3", [128, 4, S], BF16)
    kr3 = c.sb("kr3", [128, 1, S], BF16)
    krb = c.sb("krb", [128, 1, NT], BF16)
    vtok = c.sb("vtok", [128, 32, 4, 128], BF16)
    vst = c.sb("vst", [128, NT], F32)
    rot = c.sb("rot", [128, NT], BF16)
    rotb = c.sb("rotb", [128, NT], BF16)
    cosb = c.sb("cosb", [128, NT], BF16)
    sinb = c.sb("sinb", [128, NT], BF16)
    cxb = c.sb("cxb", [128, 2, NT], BF16)
    gnq = c.sb("gnq", [128, 8], F32)
    gnkv = c.sb("gnkv", [128, 4], F32)
    pst = [c.pslin[2][:, 0:128], c.pslin[3][:, 0:128]]

    with nc.Block() as blk:
        @blk.sync
        def _(sy):
            c.inc(sy.dma_start(gnq[:, :], qnw[:, :]), "init", 16)
            c.inc(sy.dma_start(gnkv[:, :], kvnw[:, :]), "init", 16)

        @blk.vector
        def _(v):
            v.wait_ge(c.sem("init"), c.cnt["init"])

    def rope_pair(t0, src3, nch, dst3, sem_pref):
        with nc.Block() as blk:
            base_sw = c.cnt.get(sem_pref + "_sw", 0)
            base_vq = c.cnt.get(sem_pref + "_vq", 0)

            @blk.gpsimd
            def _(g):
                for ch in range(nch):
                    buf = [rot, rotb][ch % 2]
                    if ch >= 2:
                        g.wait_ge(c.sem(sem_pref + "_vq"), base_vq + ch - 1)
                    ins = g.dma_start(buf[0:128:2, 0:NT], src3[1:128:2, ch, 0:NT])
                    c.inc(ins, sem_pref + "_sw", 16)
                    ins = g.dma_start(buf[1:128:2, 0:NT], src3[0:128:2, ch, 0:NT])
                    c.inc(ins, sem_pref + "_sw", 16)

            @blk.vector
            def _(v):
                for ch in range(nch):
                    buf = [rot, rotb][ch % 2]
                    v.wait_ge(c.sem(sem_pref + "_sw"), base_sw + 32 * (ch + 1))
                    v.tensor_tensor(buf[:, 0:NT], buf[:, 0:NT],
                                    sinb[:, 0:NT], ALU.mult)
                    v.tensor_tensor(c.nrm_tmp[:, 0:NT], src3[:, ch, 0:NT],
                                    cosb[:, 0:NT], ALU.mult)
                    ins = v.tensor_tensor(dst3[:, ch, t0:t0 + NT],
                                          c.nrm_tmp[:, 0:NT],
                                          buf[:, 0:NT], ALU.add)
                    c.inc(ins, sem_pref + "_vq", 1)

    for tt in range(S // NT):
        t0 = tt * NT
        with nc.Block() as blk:
            @blk.gpsimd
            def _(g):
                ins = g.dma_start(xf[:, :, 0:NT],
                                  xin[:, t0:t0 + NT].rearrange(
                                      "(k p) t -> p k t", p=128))
                c.inc(ins, "xld", 16)
                c.inc(g.dma_start(cosb[:, 0:NT], cosi[:, t0:t0 + NT]), "xld", 16)
                c.inc(g.dma_start(sinb[:, 0:NT], ssini[:, t0:t0 + NT]), "xld", 16)
                g.wait_ge(c.sem("xld"), c.cnt["xld"])
        emit_norm(c, xf, 0, NT, KD, kind="ln", eps=1e-5, dst3=xn)
        def ev_qc(eng, oc, psum):
            return eng.activation(qcb[:, oc, 0:NT], psum, AF.Identity)
        emit_linear(c, wqd_t, xn, KD, 8, NT, evac_cb=ev_qc)
        emit_norm(c, qcb, 0, NT, 8, kind="rms", eps=1e-6, dst3=qcn, gamma=gnq,
                  beta=None)
        def ev_kvc(eng, oc, psum):
            return eng.activation(kvcb[:, oc, 0:NT], psum, AF.Identity)
        emit_linear(c, wkvd_t, xn, KD, 4, NT, evac_cb=ev_kvc)
        emit_norm(c, kvcb, 0, NT, 4, kind="rms", eps=1e-6, dst3=kvcn, gamma=gnkv,
                  beta=None)
        def ev_kr(eng, oc, psum):
            return eng.activation(krb[:, 0, 0:NT], psum, AF.Identity)
        emit_linear(c, wkr_t, xn, KD, 1, NT, evac_cb=ev_kr)
        rope_pair(t0, krb, 1, kr3, "rk")
        with nc.Block() as blk:
            @blk.gpsimd
            def _(g):
                c.inc(g.dma_start(kr3[64:128, 0, t0:t0 + NT],
                                  kr3[0:64, 0, t0:t0 + NT]), "kdup", 16)
                g.wait_ge(c.sem("kdup"), c.cnt["kdup"])
        emit_linear(c, wqu_t, qcn, 8, 4, NT, dst3=qn3, dst_t0=t0)
        def ev_qp(eng, oc, psum):
            return eng.activation(qpb[:, oc, 0:NT], psum, AF.Identity)
        emit_linear(c, wqr_t, qcn, 8, 2, NT, evac_cb=ev_qp)
        rope_pair(t0, qpb, 2, qpe3, "rq")
        for hh in range(4):
            def ev_kn(eng, oc, psum, hh=hh):
                return eng.activation(kn3[:, hh, t0:t0 + NT], psum, AF.Identity)
            emit_linear(c, wkvu_t, kvcn, 4, 1, NT, OC0=2 * hh, evac_cb=ev_kn)
            def ev_v(eng, oc, psum, hh=hh):
                return eng.activation(vst[:, 0:NT], psum, AF.Identity)
            emit_linear(c, wkvu_t, kvcn, 4, 1, NT, OC0=2 * hh + 1, evac_cb=ev_v)
            with nc.Block() as blk:
                b_tr = c.cnt.get("vtr", 0)
                b_ev = c.cnt.get("vev", 0)

                @blk.tensor
                def _(te):
                    for s4 in range(4):
                        if s4 >= 2:
                            te.wait_ge(c.sem("vev"), b_ev + s4 - 1)
                        ins = te.transpose(pst[s4 % 2],
                                           vst[:, s4 * 128:(s4 + 1) * 128],
                                           c.ident_f[:])
                        c.inc(ins, "vtr", 1)

                @blk.vector
                def _(v):
                    for s4 in range(4):
                        v.wait_ge(c.sem("vtr"), b_tr + s4 + 1)
                        ins = v.tensor_copy(vtok[:, tt * 4 + s4, hh, :],
                                            pst[s4 % 2])
                        c.inc(ins, "vev", 1)

    def score_mm(te, h, kc, qt, ps):
        b0 = (h % 2) * 64
        te.matmul(ps, kn3[:, h, kc * 128:(kc + 1) * 128],
                  qn3[:, h, qt * NT:(qt + 1) * NT], start=True, stop=False)
        return te.matmul(ps, kr3[b0:b0 + 64, 0, kc * 128:(kc + 1) * 128],
                         qpe3[b0:b0 + 64, h // 2, qt * NT:(qt + 1) * NT],
                         start=False, stop=True)

    def pv_lhsT(h, kc):
        return vtok[:, kc, h, :]

    def ctx_store(v, h, qt, ps_ctx, bc):
        j = h * 8 + qt
        tmp = c.nrm_tmp[:, 0:NT]
        v.tensor_tensor(tmp, ps_ctx, bc, ALU.mult)
        return v.tensor_copy(cxb[:, j % 2, 0:NT], tmp)

    def ctx_dma(g, j, h, qt):
        return g.dma_start(ctx_out[h * 128:(h + 1) * 128, qt * NT:(qt + 1) * NT],
                           cxb[:, j % 2, 0:NT])

    emit_attn_core(c, 4, 1.0 / float(np.sqrt(192.0)), score_mm, pv_lhsT,
                   ctx_store, ctx_dma=ctx_dma)
    return c.close()


def run_l2a(XT, inp):
    nc = build_l2a()
    ROPE = 64
    inv = 1.0 / 10000.0 ** (np.arange(0, ROPE, 2, dtype=np.float32) / ROPE)
    fr = np.arange(S, dtype=np.float32)[:, None] * inv[None, :]
    cos = np.cos(fr); sin = np.sin(fr)
    cosi1 = np.repeat(cos.T, 2, axis=0)
    sini1 = np.repeat(sin.T, 2, axis=0)
    ssini1 = sini1.copy()
    ssini1[0::2] = -sini1[0::2]
    cosi = np.ascontiguousarray(np.concatenate([cosi1, cosi1], 0).astype(np.float32))
    ssini = np.ascontiguousarray(np.concatenate([ssini1, ssini1], 0).astype(np.float32))
    wkr_pad = np.zeros((D, 128), np.float32)
    wkr_pad[:, 0:64] = inp["L2_attn_wk_rope"].T
    in_maps = []
    for cid in range(NCORES):
        b, j = cid // 4, cid % 4
        in_maps.append({
            "xin": np.ascontiguousarray(XT[:, b * S:(b + 1) * S]),
            "wqd_t": np.ascontiguousarray(inp["L2_attn_wq_down"].T),
            "qnw": _ppart(inp["L2_attn_q_norm_w"]),
            "wqu_t": np.ascontiguousarray(
                inp["L2_attn_wq_up"][j * 512:(j + 1) * 512, :].T),
            "wqr_t": np.ascontiguousarray(
                inp["L2_attn_wq_rope"][j * 256:(j + 1) * 256, :].T),
            "wkvd_t": np.ascontiguousarray(inp["L2_attn_wkv_down"].T),
            "kvnw": _ppart(inp["L2_attn_kv_norm_w"]),
            "wkvu_t": np.ascontiguousarray(
                inp["L2_attn_wkv_up"][j * 1024:(j + 1) * 1024, :].T),
            "wkr_t": wkr_pad,
            "cosi": cosi, "ssini": ssini,
        })
    res = run_bass_kernel_spmd(nc, in_maps, core_ids=list(range(NCORES)))
    import ml_dtypes
    ctxT = np.zeros((D, NTOK), dtype=ml_dtypes.bfloat16)
    for cid in range(NCORES):
        b, j = cid // 4, cid % 4
        ctxT[j * 512:(j + 1) * 512, b * S:(b + 1) * S] = res.results[cid]["ctx"]
    return ctxT


def build_l3a():
    c = C("l3a")
    nc = c.nc
    xin = c.din("xin", [D, S])
    wq = c.din("wq", [D, 512])
    wk = c.din("wk", [D, 512])
    wv = c.din("wv", [D, 512])
    betas = c.din("betas", [128, 4])
    att_out = c.dout("att", [512, S], BF16)

    make_consts(c)
    make_lin_pool(c, kin_max=16, npsum=4)
    make_norm_pool(c)
    make_attn_pool(c)
    SC = 1.0 / float(np.sqrt(128.0))

    xn = c.sb("xn", [128, KD, NT], BF16)
    xf = xn
    q3 = c.sb("q3", [128, 4, S], BF16)
    k3 = c.sb("k3", [128, 4, S], BF16)
    vtok = c.sb("vtok", [128, 8, 4, 4, 128], BF16)
    vst = c.sb("vst", [128, NT], F32)
    sA = c.sb("sA", [128, NT], F32)
    sB = c.sb("sB", [128, NT], F32)
    sq_bf = c.sb("sq_bf", [128, NT], BF16)
    sk_f = c.sb("sk_f", [128, NT], F32)
    sk_bf = c.sb("sk_bf", [128, NT], BF16)
    sk_tok = c.sb("sk_tok", [128, 4, 128], BF16)
    dlt = c.sb("dlt", [128, 4, 128], BF16)
    am_t = c.sb("am_t", [128, NT], BF16)
    ad_t = c.sb("ad_t", [128, NT], BF16)
    t1b = c.sb("t1b", [128, NT], F32)
    t2b = c.sb("t2b", [128, NT], F32)
    attb = c.sb("attb", [128, NT], BF16)
    attp = c.sb("attp", [128, 2, NT], BF16)
    memf = c.sb("memf", [128, 4, 128], F32)
    membf = c.sb("membf", [128, 4, 128], BF16)
    zf = c.sb("zf", [128, 4], F32)
    zbf = c.sb("zbf", [128, 4], BF16)
    szf = c.sb("szf", [128, 4], F32)
    bsg = c.sb("bsg", [128, 4], F32)
    b1m = c.sb("b1m", [128, 4], F32)
    rowr = c.sb("rowr", [1, NT], F32)
    pst = [c.pslin[2][:, 0:128], c.pslin[3][:, 0:128]]

    with nc.Block() as blk:
        @blk.sync
        def _(sy):
            c.inc(sy.dma_start(c.nrm_tmp[:, 0:4], betas[:, :]), "init", 16)

        @blk.scalar
        def _(sc_):
            sc_.wait_ge(c.sem("init"), c.cnt["init"])
            sc_.activation(bsg[:, :], c.nrm_tmp[:, 0:4], AF.Sigmoid)
    with nc.Block() as blk:
        @blk.vector
        def _(v):
            v.tensor_scalar(b1m[:, :], bsg[:, :], -1.0, 1.0, ALU.mult, ALU.add)

    for tt in range(S // NT):
        t0 = tt * NT
        with nc.Block() as blk:
            @blk.gpsimd
            def _(g):
                ins = g.dma_start(xf[:, :, 0:NT],
                                  xin[:, t0:t0 + NT].rearrange(
                                      "(k p) t -> p k t", p=128))
                c.inc(ins, "xld", 16)
                g.wait_ge(c.sem("xld"), c.cnt["xld"])
        emit_norm(c, xf, 0, NT, KD, kind="ln", eps=1e-5, dst3=xn)
        emit_linear(c, wq, xn, KD, 4, NT, dst3=q3, dst_t0=t0)
        emit_linear(c, wk, xn, KD, 4, NT, dst3=k3, dst_t0=t0)
        for hh in range(4):
            def ev_v(eng, oc, psum, hh=hh):
                return eng.activation(vst[:, 0:NT], psum, AF.Identity)
            emit_linear(c, wv, xn, KD, 1, NT, OC0=hh, evac_cb=ev_v)
            with nc.Block() as blk:
                b_tr = c.cnt.get("vtr", 0)
                b_ev = c.cnt.get("vev", 0)

                @blk.tensor
                def _(te):
                    for s4 in range(4):
                        if s4 >= 2:
                            te.wait_ge(c.sem("vev"), b_ev + s4 - 1)
                        ins = te.transpose(pst[s4 % 2],
                                           vst[:, s4 * 128:(s4 + 1) * 128],
                                           c.ident_f[:])
                        c.inc(ins, "vtr", 1)

                @blk.vector
                def _(v):
                    for s4 in range(4):
                        v.wait_ge(c.sem("vtr"), b_tr + s4 + 1)
                        ins = v.tensor_copy(vtok[:, tt, hh, s4, :], pst[s4 % 2])
                        c.inc(ins, "vev", 1)

    for hh in range(4):
        with nc.Block() as blk:
            @blk.vector
            def _(v):
                v.memset(memf[:, hh, :], 0.0)
                v.memset(membf[:, hh, :], 0.0)
                v.memset(zf[:, hh:hh + 1], 1.0 / 128.0)
                v.memset(zbf[:, hh:hh + 1], 1.0 / 128.0)
        for t in range(8):
            qq = q3[:, hh, t * NT:(t + 1) * NT]
            kk = k3[:, hh, t * NT:(t + 1) * NT]
            with nc.Block() as blk:
                b_s1 = c.cnt.get("d_s1", 0)
                b_s2 = c.cnt.get("d_s2", 0)

                @blk.vector
                def _(v):
                    ins = v.tensor_scalar_min(sA[:, 0:NT], qq, 0.0)
                    c.inc(ins, "d_s1")
                    v.wait_ge(c.sem("d_s2"), b_s2 + 1)
                    v.scalar_tensor_tensor(sq_bf[:, 0:NT], qq, 0.0, sB[:, 0:NT],
                                           ALU.max, ALU.add)
                    ins = v.tensor_scalar_min(sA[:, 0:NT], kk, 0.0)
                    c.inc(ins, "d_s1")
                    v.wait_ge(c.sem("d_s2"), b_s2 + 2)
                    ins = v.scalar_tensor_tensor(sk_f[:, 0:NT], kk, 0.0,
                                                 sB[:, 0:NT], ALU.max, ALU.add)
                    c.inc(ins, "vss"); v.wait_ge(c.sem("vss"), c.cnt["vss"])
                    v.tensor_copy(sk_bf[:, 0:NT], sk_f[:, 0:NT])

                @blk.scalar
                def _(sc_):
                    for rep in range(2):
                        sc_.wait_ge(c.sem("d_s1"), b_s1 + rep + 1)
                        ins = sc_.activation(sB[:, 0:NT], sA[:, 0:NT], AF.Exp)
                        c.inc(ins, "d_s2")
            with nc.Block() as blk:
                b_tr = c.cnt.get("vtr", 0)
                b_ev = c.cnt.get("vev", 0)

                @blk.tensor
                def _(te):
                    for s4 in range(4):
                        if s4 >= 2:
                            te.wait_ge(c.sem("vev"), b_ev + s4 - 1)
                        ins = te.transpose(pst[s4 % 2],
                                           sk_f[:, s4 * 128:(s4 + 1) * 128],
                                           c.ident_f[:])
                        c.inc(ins, "vtr", 1)

                @blk.vector
                def _(v):
                    for s4 in range(4):
                        v.wait_ge(c.sem("vtr"), b_tr + s4 + 1)
                        ins = v.tensor_copy(sk_tok[:, s4, :], pst[s4 % 2])
                        c.inc(ins, "vev", 1)
            with nc.Block() as blk:
                b_mm = c.cnt.get("d_mm", 0)
                b_ex = c.cnt.get("d_ex", 0)
                b_mk = c.cnt.get("d_mk", 0)

                @blk.tensor
                def _(te):
                    for sc4 in range(4):
                        ins = te.matmul(c.at_psS[sc4 % 2][:, 0:NT],
                                        kk[:, sc4 * 128:(sc4 + 1) * 128]
                                        if False else
                                        k3[:, hh, t * NT + sc4 * 128:
                                           t * NT + (sc4 + 1) * 128],
                                        qq, start=True, stop=True)
                        c.inc(ins, "d_mm", 1)
                        te.wait_ge(c.sem("d_mk"), b_mk + sc4 + 1)
                        te.matmul(c.at_den[:, 0:NT], c.ones_bf[:],
                                  c.at_pt[sc4 % 3][:, 0:NT],
                                  start=(sc4 == 0), stop=(sc4 == 3))
                        ins = te.matmul(c.at_ctx[:, 0:NT], vtok[:, t, hh, sc4, :],
                                        c.at_pt[sc4 % 3][:, 0:NT],
                                        start=(sc4 == 0), stop=(sc4 == 3))
                        c.inc(ins, "d_pv", 1)
                    ins = te.matmul(c.at_psb[:, 0:NT], membf[:, hh, :], sq_bf[:, 0:NT],
                                    start=True, stop=True)
                    c.inc(ins, "d_am", 1)
                    ins = te.matmul(c.ps_rowB[:, 0:NT], zbf[:, hh:hh + 1],
                                    sq_bf[:, 0:NT], start=True, stop=True)
                    c.inc(ins, "d_az", 1)

                @blk.scalar
                def _(sc_):
                    for sc4 in range(4):
                        sc_.wait_ge(c.sem("d_mm"), b_mm + sc4 + 1)
                        ins = sc_.activation(c.at_pt[sc4 % 3][:, 0:NT],
                                             c.at_psS[sc4 % 2][:, 0:NT], AF.Exp,
                                             scale=SC)
                        c.inc(ins, "d_ex", 1)

                @blk.vector
                def _(v):
                    for sc4 in range(4):
                        v.wait_ge(c.sem("d_ex"), b_ex + sc4 + 1)
                        ins = v.tensor_tensor(c.at_pt[sc4 % 3][:, 0:NT],
                                              c.at_pt[sc4 % 3][:, 0:NT],
                                              c.at_msk[:, sc4, 0:NT], ALU.mult)
                        c.inc(ins, "d_mk", 1)
            with nc.Block() as blk:
                @blk.vector
                def _(v):
                    v.tensor_copy(am_t[:, 0:NT], c.at_psb[:, 0:NT])
                    ins = v.tensor_copy(rowr[:, 0:NT], c.ps_rowB[:, 0:NT])
                    c.inc(ins, "vss"); v.wait_ge(c.sem("vss"), c.cnt["vss"])
                    ins = v.reciprocal(rowr[:, 0:NT], rowr[:, 0:NT])
                    c.inc(ins, "d_rc")

                @blk.tensor
                def _(te):
                    te.wait_ge(c.sem("d_rc"), c.cnt["d_rc"])
                    ins = te.matmul(c.at_psb[:, 0:NT], c.ones_f1[:], rowr[:, 0:NT],
                                    start=True, stop=True)
                    c.inc(ins, "d_bc", 1)
            with nc.Block() as blk:
                @blk.vector
                def _(v):
                    ins = v.tensor_tensor(t1b[:, 0:NT], am_t[:, 0:NT],
                                          c.at_psb[:, 0:NT], ALU.mult)
                    c.inc(ins, "vss"); v.wait_ge(c.sem("vss"), c.cnt["vss"])
                    v.tensor_single_scalar(t1b[:, 0:NT], t1b[:, 0:NT],
                                           bsg[:, hh:hh + 1], ALU.mult)
            with nc.Block() as blk:
                @blk.tensor
                def _(te):
                    for sc4 in range(4):
                        te.matmul(c.pslin[2][:, sc4 * 128:(sc4 + 1) * 128],
                                  sk_bf[:, sc4 * 128:(sc4 + 1) * 128],
                                  membf[:, hh, :], start=(sc4 == 0),
                                  stop=(sc4 == 3))
                        ins = te.matmul(c.pslin[3][:, sc4:sc4 + 1],
                                        sk_bf[:, sc4 * 128:(sc4 + 1) * 128],
                                        zbf[:, hh:hh + 1],
                                        start=(sc4 == 0), stop=(sc4 == 3))
                        c.inc(ins, "d_sm", 1)
            with nc.Block() as blk:
                @blk.vector
                def _(v):
                    ins = v.tensor_copy(szf[:, 0:4], c.pslin[3][:, 0:4])
                    c.inc(ins, "vss"); v.wait_ge(c.sem("vss"), c.cnt["vss"])
                    ins = v.reciprocal(szf[:, 0:4], szf[:, 0:4])
                    c.inc(ins, "vss"); v.wait_ge(c.sem("vss"), c.cnt["vss"])
                    for sc4 in range(4):
                        ins = v.tensor_single_scalar(
                            c.nrm_tmp[:, 0:128],
                            c.pslin[2][:, sc4 * 128:(sc4 + 1) * 128],
                            szf[:, sc4:sc4 + 1], ALU.mult)
                        c.inc(ins, "vss"); v.wait_ge(c.sem("vss"), c.cnt["vss"])
                        v.tensor_tensor(dlt[:, sc4, :], vtok[:, t, hh, sc4, :],
                                        c.nrm_tmp[:, 0:128], ALU.subtract)
            with nc.Block() as blk:
                @blk.tensor
                def _(te):
                    for sc4 in range(4):
                        te.matmul(c.pslin[3][:, 0:128], sk_tok[:, sc4, :],
                                  dlt[:, sc4, :], start=(sc4 == 0), stop=False)
                        te.matmul(c.pslin[3][:, 128:129], sk_tok[:, sc4, :],
                                  c.ones_bf[:], start=False, stop=(sc4 == 3))
            with nc.Block() as blk:
                @blk.vector
                def _(v):
                    ins = v.tensor_tensor(memf[:, hh, :], memf[:, hh, :],
                                          c.pslin[3][:, 0:128], ALU.add)
                    c.inc(ins, "vss"); v.wait_ge(c.sem("vss"), c.cnt["vss"])
                    v.tensor_copy(membf[:, hh, :], memf[:, hh, :])
                    ins = v.tensor_tensor(zf[:, hh:hh + 1], zf[:, hh:hh + 1],
                                          c.pslin[3][:, 128:129], ALU.add)
                    c.inc(ins, "vss"); v.wait_ge(c.sem("vss"), c.cnt["vss"])
                    v.tensor_copy(zbf[:, hh:hh + 1], zf[:, hh:hh + 1])
            with nc.Block() as blk:
                @blk.vector
                def _(v):
                    v.tensor_copy(ad_t[:, 0:NT], c.at_ctx[:, 0:NT])
                    ins = v.tensor_copy(rowr[:, 0:NT], c.at_den[:, 0:NT])
                    c.inc(ins, "vss"); v.wait_ge(c.sem("vss"), c.cnt["vss"])
                    ins = v.reciprocal(rowr[:, 0:NT], rowr[:, 0:NT])
                    c.inc(ins, "d_rc")

                @blk.tensor
                def _(te):
                    te.wait_ge(c.sem("d_rc"), c.cnt["d_rc"])
                    ins = te.matmul(c.ps_bB[:, 0:NT], c.ones_f1[:], rowr[:, 0:NT],
                                    start=True, stop=True)
                    c.inc(ins, "d_bc", 1)
            with nc.Block() as blk:
                @blk.vector
                def _(v):
                    ins = v.tensor_tensor(t2b[:, 0:NT], ad_t[:, 0:NT],
                                          c.ps_bB[:, 0:NT], ALU.mult)
                    c.inc(ins, "vss"); v.wait_ge(c.sem("vss"), c.cnt["vss"])
                    ins = v.scalar_tensor_tensor(t2b[:, 0:NT], t2b[:, 0:NT],
                                                 b1m[:, hh:hh + 1], t1b[:, 0:NT],
                                                 ALU.mult, ALU.add)
                    c.inc(ins, "vss"); v.wait_ge(c.sem("vss"), c.cnt["vss"])
                    ins = v.tensor_copy(attb[:, 0:NT], t2b[:, 0:NT])
                    c.inc(ins, "vss"); v.wait_ge(c.sem("vss"), c.cnt["vss"])
                    ins = v.tensor_copy(
                        attp[:, t % 2, 0:NT].rearrange("p (so sb) -> p so sb",
                                                       so=16, sb=32),
                        attb[:, 0:NT].rearrange("p (sb so) -> p so sb",
                                                sb=32, so=16))
                    c.inc(ins, "d_pm", 1)

                @blk.gpsimd
                def _(g):
                    g.wait_ge(c.sem("d_pm"), c.cnt["d_pm"])
                    ins = g.dma_start(
                        att_out[hh * 128:(hh + 1) * 128, t * NT:(t + 1) * NT],
                        attp[:, t % 2, 0:NT])
                    c.inc(ins, "d_st", 16)
                    g.wait_ge(c.sem("d_st"), c.cnt["d_st"])
    return c.close()


def run_l3a(XT, inp):
    nc = build_l3a()
    in_maps = []
    bet = inp["L3_attn_betas"][0, :, 0, :]
    for cid in range(NCORES):
        b, j = cid // 4, cid % 4
        in_maps.append({
            "xin": np.ascontiguousarray(XT[:, b * S:(b + 1) * S]),
            "wq": np.ascontiguousarray(inp["L3_attn_Wq"][:, j * 512:(j + 1) * 512]),
            "wk": np.ascontiguousarray(inp["L3_attn_Wk"][:, j * 512:(j + 1) * 512]),
            "wv": np.ascontiguousarray(inp["L3_attn_Wv"][:, j * 512:(j + 1) * 512]),
            "betas": np.ascontiguousarray(bet[j * 4:(j + 1) * 4].T),
        })
    res = run_bass_kernel_spmd(nc, in_maps, core_ids=list(range(NCORES)))
    import ml_dtypes
    attT = np.zeros((D, NTOK), dtype=ml_dtypes.bfloat16)
    for cid in range(NCORES):
        b, j = cid // 4, cid % 4
        attT[j * 512:(j + 1) * 512, b * S:(b + 1) * S] = res.results[cid]["att"]
    return attT


def build_l3a2():
    c = C("l3a2")
    nc = c.nc
    xin = c.din("xin", [D, TPC])
    ain = c.din("ain", [D, TPC], BF16)
    wo = c.din("wo", [D, D])
    xout = c.dout("xout", [D, TPC])
    make_consts(c)
    make_lin_pool(c, kin_max=16, npsum=4)
    make_norm_pool(c)
    xf = c.sb("xf", [128, KD, NT], F32)
    a4 = c.sb("a4", [128, KD, TPC], BF16)
    cb = c.sb("cb", [128, KD, TPC], BF16)
    yb = c.sb("yb", [128, KD, NT], F32)
    with nc.Block() as blk:
        @blk.sync
        def _(sy):
            ins = sy.dma_start(a4[:, :, :],
                               ain.rearrange("(h p) t -> p h t", p=128))
            c.inc(ins, "ald", 16)

        @blk.vector
        def _(v):
            v.wait_ge(c.sem("ald"), c.cnt["ald"])
    with nc.Block() as blk:
        @blk.vector
        def _(v):
            srcr = a4.rearrange("p h (tt so sb) -> p tt h so sb", tt=2, so=16,
                                sb=32)
            for jc in range(16):
                v.tensor_copy(
                    cb[:, jc, :].rearrange("p (tt h sb) -> p tt h sb",
                                           tt=2, h=16, sb=32),
                    srcr[:, :, :, jc, :])
    for tt in range(TPC // NT):
        t0 = tt * NT
        with nc.Block() as blk:
            @blk.sync
            def _(sy):
                dma_in_act(c, sy, xf, xin, t0, NT)

            @blk.vector
            def _(v):
                v.wait_ge(c.sem("xld"), c.cnt["xld"])

        def evac_y(eng, oc, psum):
            return eng.activation(yb[:, oc, 0:NT], psum, AF.Identity)
        emit_linear(c, wo, cb, KD, KD, NT, xt0=t0, evac_cb=evac_y)
        sandwich_store(c, yb, xf, xout, t0)
    return c.close()


def run_l3a2(XT, attT, wo):
    nc = build_l3a2()
    in_maps = []
    for cid in range(NCORES):
        t0 = cid * TPC
        in_maps.append({
            "xin": np.ascontiguousarray(XT[:, t0:t0 + TPC]),
            "ain": np.ascontiguousarray(attT[:, t0:t0 + TPC]),
            "wo": wo,
        })
    res = run_bass_kernel_spmd(nc, in_maps, core_ids=list(range(NCORES)))
    return np.concatenate([res.results[i]["xout"] for i in range(NCORES)], axis=1)


def kernel(**inputs):
    x = np.asarray(inputs["L0_attn_hidden_states"], np.float32)
    XT = np.ascontiguousarray(x.reshape(NTOK, D).T)
    w2pt = run_l3w(inputs)
    w_guT, w_dT = run_l1dq(inputs)
    ctxT = run_l0a(XT, inputs)
    XT = run_attn_a2("l0a2", XT, ctxT, inputs["L0_attn_w_dense"])
    XT = run_l0f(XT, inputs)
    ctxT = run_l1a(XT, inputs)
    XT = run_attn_a2("l1a2", XT, ctxT, inputs["L1_attn_wo"])
    XT = run_l1f(XT, inputs, w_guT, w_dT)
    ctxT = run_l2a(XT, inputs)
    XT = run_attn_a2("l2a2", XT, ctxT,
                     np.ascontiguousarray(inputs["L2_attn_wo"].T))
    XT = run_l2f(XT, inputs)
    attT = run_l3a(XT, inputs)
    XT = run_l3a2(XT, attT, inputs["L3_attn_Wo"])
    XT = run_l3f(XT, inputs, w2pt)
    return np.ascontiguousarray(XT.T).reshape(2, S, D).astype(np.float32)
